# version 09
# baseline (speedup 1.0000x reference)
import numpy as np
import concourse.bass as bass
import concourse.mybir as mybir
from concourse.bass_utils import run_bass_kernel_spmd
from contextlib import ExitStack

F32 = mybir.dt.float32
BF16 = mybir.dt.bfloat16
U8 = mybir.dt.uint8
AF = mybir.ActivationFunctionType
ALU = mybir.AluOpType

L = 4096
DM = 1024
DH = 1024
NST = 16
DTR = 64
TBA = 512
TBB = 1024
TBC = 512
EPS = 1e-5
PAIRS = [[0, 1], [2, 3], [4, 5], [6, 7]]

DEV = 0
BUILD_UPTO = "C"

ENGINES = ("sync", "vector", "scalar", "tensor", "gpsimd")


class Plan:
    def __init__(self):
        self.q = {e: [] for e in ENGINES}
        self.cnt = {}
        self.sems = {}

    def add_sem(self, name, handle):
        self.sems[name] = handle
        self.cnt[name] = 0

    def em(self, eng, fn, inc=None, n=1):
        if inc is None:
            self.q[eng].append((fn, None, 0))
            return None
        self.cnt[inc] += n
        self.q[eng].append((fn, inc, n))
        return self.cnt[inc]

    def wait(self, eng, sem, val):
        if val is None or val <= 0:
            return
        self.q[eng].append((None, sem, val))

    def replay(self, eng, e):
        for fn, sem, val in self.q[eng]:
            if fn is None:
                e.wait_ge(self.sems[sem], val)
            else:
                ins = fn(e)
                if sem is not None:
                    ins.then_inc(self.sems[sem], val)


class Arena:
    def __init__(self, t, cols):
        self.t, self.cols, self.off = t, cols, 0

    def alloc(self, n):
        n = (n + 15) // 16 * 16
        assert self.off + n <= self.cols, f"arena overflow {self.off}+{n}>{self.cols}"
        o = self.off
        self.off += n
        return o

    def v(self, o, n):
        return self.t[:, o:o + n]

    def r(self, o, n):
        return self.t[0:1, o:o + n]

    def mark(self):
        return self.off

    def reset(self, m):
        self.off = m


def build_nc(dev=0, upto="C", cstop=99):
    nc = bass.Bass("TRN2", target_bir_lowering=False,
                   detect_race_conditions=False)

    def din(name, shape, dt=F32):
        return nc.dram_tensor(name, shape, dt, kind="ExternalInput").ap()

    def dout(name, shape, dt=F32):
        return nc.dram_tensor(name, shape, dt, kind="ExternalOutput").ap()

    def dint(name, shape, dt):
        return nc.dram_tensor(name, shape, dt).ap()

    x_f = din("x_f", [DM, L])
    x_rs1 = din("x_rs1", [DM, TBB])
    x_rs2 = din("x_rs2", [DM, TBB])
    wi_t = din("wi_t", [DM, 2 * DH])
    wx_t = din("wx_t", [DH, 96])
    wdt_t = din("wdt_t", [DTR, DH])
    wo_t = din("wo_t", [DH, DM])
    convw = din("convw", [DH, 4])
    convb = din("convb", [128, 8])
    bdt_t = din("bdt_t", [128, 8])
    alog = din("alog", [128, 128])
    dp_t = din("dp_t", [128, 8])
    gam = din("gam", [128, 8])
    bet = din("bet", [128, 8])
    fc1b = din("fc1b", [128, 32])
    fc2b = din("fc2b", [128, 8])
    fc1q = din("fc1q", [DM, 4096], U8)
    fc2q = din("fc2q", [4096, DM], U8)
    fc1s = din("fc1s", [8, 4096])
    fc2s = din("fc2s", [32, DM])
    ones1 = din("ones1", [1, 128])
    onesc = din("onesc", [128, 1])

    x2a = dout("x2a", [DM, TBB])
    x2b = dout("x2b", [DM, TBB])
    if dev >= 1:
        dv_xp = dout("dv_xp", [DH, L], BF16)
        dv_z = dout("dv_z", [DH, L], BF16)
        dv_xdbl = dout("dv_xdbl", [96, L])
    if dev >= 2:
        dv_y = dout("dv_y", [DH, L])
        dv_rs1 = dout("dv_rs1", [DM, TBB], BF16)
        dv_rs2 = dout("dv_rs2", [DM, TBB], BF16)
    if dev >= 3:
        dv_dl = dout("dv_dl", [128, 8 * TBB], BF16)
        dv_du = dout("dv_du", [128, 8 * TBB], BF16)
        dv_y0 = dout("dv_y0", [128, 8 * TBB], F32)
        dv_cr = dout("dv_cr", [128, 128], F32)
        dv_bg = dout("dv_bg", [128, 4 * TBB], BF16)
        dv_bg2 = dout("dv_bg2", [96, TBB], BF16)
        dv_fg = dout("dv_fg", [128, TBB], F32)
        dv_wdt = dout("dv_wdt", [DTR, DH], BF16)

    w1_d = dint("w1_d", [DM, 4096], BF16)
    w2_d = dint("w2_d", [4096, DM], BF16)
    z_d = dint("z_d", [DH, L], BF16)
    xp_d = dint("xp_d", [DH, L], BF16)
    xdbl_p = dint("xdbl_p", [96, L], F32)
    xdbl_s = dint("xdbl_s", [96, L], F32)
    rs1in = dint("rs1in", [2 * DM, TBB], BF16)
    rs2in = dint("rs2in", [2 * DM, TBB], BF16)
    rs1out = dint("rs1out", [DM, TBB], BF16)
    rs2out = dint("rs2out", [DM, TBB], BF16)

    p = Plan()
    es = ExitStack()
    with es:
        AF32C = 17600
        ABFC = 38048
        af32_t = es.enter_context(nc.sbuf_tensor("af32", [128, AF32C], F32))
        abf_t = es.enter_context(nc.sbuf_tensor("abf", [128, ABFC], BF16))
        a32 = Arena(af32_t, AF32C)
        abf = Arena(abf_t, ABFC)

        wo_bf = es.enter_context(nc.sbuf_tensor("wo_bf", [128, 8 * DM], BF16))
        wx_bf = es.enter_context(nc.sbuf_tensor("wx_bf", [128, 8 * 96], BF16))
        wdt_f = es.enter_context(nc.sbuf_tensor("wdt_f", [DTR, DH], F32))
        wdt_bf = es.enter_context(nc.sbuf_tensor("wdt_bf", [DTR, DH], BF16))
        small = es.enter_context(nc.sbuf_tensor("small", [128, 256], F32))
        negA = 80
        cwo = 208
        carry = es.enter_context(nc.sbuf_tensor("carry", [128, 128], F32))
        onebf = es.enter_context(nc.sbuf_tensor("onebf", [1, 128], BF16))
        onecbf = es.enter_context(nc.sbuf_tensor("onecbf", [128, 1], BF16))
        alog_sb = es.enter_context(nc.sbuf_tensor("alog_sb", [128, 128], F32))
        qbuf = es.enter_context(nc.sbuf_tensor("qbuf", [128, 4096], U8))
        xdblb = es.enter_context(nc.sbuf_tensor("xdblb", [96, L], BF16))

        pbank = [es.enter_context(nc.psum_tensor(f"pb{i}", [128, 512], F32))
                 for i in range(8)]
        pA, pB, pC_, pD, pE, pF, pG, pH = pbank

        DMA_SEMS = ("dW", "dR0", "dR1", "dQ", "dQS", "dX0", "dX1", "dZS",
                    "dXD", "dDV", "dXC0", "dXC1", "dXP", "dBC0", "dBC1",
                    "dZL", "dOG", "dMS", "dF0", "dF1", "dF2", "dF3",
                    "dW20", "dW21", "dX2")
        for s in DMA_SEMS + ("V", "S", "T", "P", "CC"):
            p.add_sem(s, es.enter_context(nc.semaphore(s)))

        def dma(q, sem, out, in_):
            return p.em(q, lambda e, o=out, i=in_: e.dma_start(out=o, in_=i),
                        inc=sem, n=16)

        def vv(fn, inc=False):
            return p.em("vector", fn, "V" if inc else None)

        def ss(fn, inc=False):
            return p.em("scalar", fn, "S" if inc else None)

        def tt(fn, inc=False):
            return p.em("tensor", fn, "T" if inc else None)

        def gg(fn, inc=False):
            return p.em("gpsimd", fn, "P" if inc else None)

        def w(eng, sem, val):
            p.wait(eng, sem, val)

        cstt = es.enter_context(nc.sbuf_tensor("cstt", [128, 4], F32))
        CONSTS = [-128.0, 1.0, EPS, 0.0]
        for i, val in enumerate(CONSTS):
            nc.const_aps.register(F32, val, cstt[:, i:i + 1])
        vcst = None
        for i, val in enumerate(CONSTS):
            vcst = p.em("vector", lambda e, i=i, val=val, t=cstt: e.memset(
                t[:, i:i + 1], val), "V", 1)
        p.wait("scalar", "V", vcst)
        p.wait("tensor", "V", vcst)

        def phase_barrier():
            for eng in ENGINES:
                for s in ("V", "S", "T", "P"):
                    w(eng, s, p.cnt[s])
                for s in DMA_SEMS:
                    w(eng, s, p.cnt[s])

        o_stw = a32.alloc(4096)
        o_sbf = abf.alloc(2048)
        base32, basebf = a32.mark(), abf.mark()
        o_wi = abf.alloc(8 * 2 * DH)
        basebf_wi = abf.mark()

        def wi_bf_sl(a, b):
            return abf_t[:, o_wi + a:o_wi + b]

        dma("sync", "dW", small[:, 0:8], convb)
        dma("sync", "dW", small[:, 8:16], bdt_t)
        dma("sync", "dW", small[:, 16:24], dp_t)
        dma("sync", "dW", small[:, 24:32], gam)
        dma("sync", "dW", small[:, 32:40], bet)
        dma("sync", "dW", small[:, 40:72], fc1b)
        dma("sync", "dW", small[:, 72:80], fc2b)
        dma("sync", "dW", alog_sb[:, :], alog)
        for dt in range(8):
            dma("sync", "dW", small[:, cwo + 4 * dt:cwo + 4 * (dt + 1)],
                convw[128 * dt:128 * (dt + 1), :])
        dma("sync", "dW", wdt_f[:, :], wdt_t)
        dma("sync", "dW", small[:, 250:251], onesc)
        o_r1 = a32.alloc(128)
        c1 = dma("sync", "dW", a32.r(o_r1, 128), ones1)
        w("scalar", "dW", c1)
        ss(lambda e: e.copy(out=onebf[:, :], in_=a32.r(o_r1, 128)))
        ss(lambda e: e.copy(out=onecbf[:, :], in_=small[:, 250:251]))
        ss(lambda e: e.copy(out=wdt_bf[:, :], in_=wdt_f[:, :]))
        ss(lambda e: e.activation(out=small[:, 72:80], in_=small[:, 72:80],
                                  func=AF.Copy, scale=0.5))
        s1 = ss(lambda e: e.activation(out=small[:, negA:negA + 128],
                                       in_=alog_sb[:, :], func=AF.Exp), True)
        w("vector", "S", s1)
        v1 = vv(lambda e: e.tensor_scalar(
            out=small[:, negA:negA + 128], in0=small[:, negA:negA + 128],
            scalar1=-1.0, scalar2=None, op0=ALU.mult), True)
        w("scalar", "V", v1)

        o_ws = a32.alloc(2 * 2048)
        wjobs = ([(wx_t, kt, 96, lambda kt: wx_bf[:, 96 * kt:96 * (kt + 1)])
                  for kt in range(8)] +
                 [(wo_t, kt, DM, lambda kt: wo_bf[:, DM * kt:DM * (kt + 1)])
                  for kt in range(8)] +
                 [(wi_t, kt, 2 * DH,
                   lambda kt: wi_bf_sl(2 * DH * kt, 2 * DH * (kt + 1)))
                  for kt in range(8)])
        scasts = {}
        for gi, (srcd, kt, cols, dstf) in enumerate(wjobs):
            if gi >= 2:
                w("sync", "S", scasts[gi - 2])
            cw1 = dma("sync", f"dR{gi % 2}", a32.v(o_ws + (gi % 2) * 2048, cols),
                      srcd[128 * kt:128 * (kt + 1), :])
            w("scalar", f"dR{gi % 2}", cw1)
            scasts[gi] = ss(lambda e, kt=kt, cols=cols, dstf=dstf,
                            o=o_ws + (gi % 2) * 2048: e.copy(
                out=dstf(kt), in_=a32.v(o, cols)), True)

        o_scr = a32.alloc(4096)
        o_srb = abf.alloc(4096)
        o_qm = abf.alloc(4096)
        o_wr = abf.alloc(4096)

        def dequant(qdram, sdram, wdram, nkt, ncol, s2=None, sdma=None,
                    vlast=None):
            vprev = {}
            for kt in range(nkt):
                if s2 is not None:
                    w("gpsimd", "S", s2)
                cq = dma("gpsimd", "dQ", qbuf[:, 0:ncol],
                         qdram[128 * kt:128 * (kt + 1), :])
                cs = dma("gpsimd", "dQ", a32.r(o_scr, ncol), sdram[kt:kt + 1, :])
                w("scalar", "dQ", cs)
                if vlast is not None:
                    w("scalar", "V", vlast)
                ss(lambda e, n=ncol: e.copy(out=abf.r(o_srb, n), in_=a32.r(o_scr, n)))
                s2 = ss(lambda e, n=ncol: e.activation(
                    out=abf.v(o_qm, n), in_=qbuf[:, 0:n], func=AF.Identity,
                    bias=-128.0, scale=1.0), True)
                if sdma is not None:
                    w("vector", "dQS", sdma)
                for ch in range(ncol // 512):
                    bank = pbank[ch % 2]
                    if (kt, ch) != (0, 0):
                        pv = vprev.get((kt, ch - 2)) or vprev.get(
                            (kt - 1, ncol // 512 - 2 + ch))
                        w("tensor", "V", pv)
                    if ch == 0:
                        w("tensor", "S", s2)
                    t1 = tt(lambda e, b=bank, ch=ch: e.matmul(
                        b[:, :], onebf[:, :], abf.r(o_srb + 512 * ch, 512),
                        start=True, stop=True), True)
                    w("vector", "T", t1)
                    w("vector", "S", s2)
                    vlast = vv(lambda e, b=bank, ch=ch: e.tensor_tensor(
                        out=abf.v(o_wr + 512 * ch, 512),
                        in0=abf.v(o_qm + 512 * ch, 512), in1=b[:, :],
                        op=ALU.mult), True)
                    vprev[(kt, ch)] = vlast
                w("sync", "V", vlast)
                sdma = dma("sync", "dQS", wdram[128 * kt:128 * (kt + 1), :],
                           abf.v(o_wr, ncol))
            return s2, sdma, vlast

        dq_state = dequant(fc1q, fc1s, w1_d, 8, 4096)
        dequant(fc2q, fc2s, w2_d, 32, DM, *dq_state)
        phase_barrier()
        a32.reset(base32)
        abf.reset(basebf)

        def ln_stats(get_x, tb, o_sq, D=1024):
            nch = tb // 512
            w("tensor", "V", p.cnt["V"])
            w("tensor", "S", p.cnt["S"])
            t1 = None
            for ch in range(nch):
                bs, bq = (pE, pF) if ch == 0 else (pG, pH)
                tq_prev = None
                for kt in range(8):
                    if tq_prev is not None:
                        w("scalar", "T", tq_prev)
                    s1 = ss(lambda e, kt=kt, ch=ch: e.activation(
                        out=abf.v(o_sq + (kt % 2) * 512, 512),
                        in_=get_x(kt)[:, 512 * ch:512 * (ch + 1)],
                        func=AF.Square), True)
                    w("tensor", "S", s1)
                    tt(lambda e, kt=kt, ch=ch, bs=bs: e.matmul(
                        bs[0:1, :], onecbf[:, :],
                        get_x(kt)[:, 512 * ch:512 * (ch + 1)],
                        start=(kt == 0), stop=(kt == 7)))
                    t1 = tt(lambda e, kt=kt, bq=bq: e.matmul(
                        bq[0:1, :], onecbf[:, :], abf.v(o_sq + (kt % 2) * 512, 512),
                        start=(kt == 0), stop=(kt == 7)), True)
                    if kt >= 1:
                        tq_prev = t1
            w("vector", "T", t1)
            vlast = None
            for ch in range(nch):
                bs, bq = (pE, pF) if ch == 0 else (pG, pH)
                o = 512 * ch
                vv(lambda e, bs=bs, o=o: e.tensor_scalar(
                    out=a32.r(o_stw + o, 512), in0=bs[0:1, :], scalar1=1.0 / D,
                    scalar2=None, op0=ALU.mult))
                vv(lambda e, bq=bq, o=o: e.tensor_scalar(
                    out=a32.r(o_stw + 1024 + o, 512), in0=bq[0:1, :],
                    scalar1=1.0 / D, scalar2=None, op0=ALU.mult))
                vv(lambda e, o=o: e.tensor_tensor(
                    out=a32.r(o_stw + 2048 + o, 512), in0=a32.r(o_stw + o, 512),
                    in1=a32.r(o_stw + o, 512), op=ALU.mult))
                vlast = vv(lambda e, o=o: e.tensor_tensor(
                    out=a32.r(o_stw + 1024 + o, 512),
                    in0=a32.r(o_stw + 1024 + o, 512),
                    in1=a32.r(o_stw + 2048 + o, 512), op=ALU.subtract), True)
            w("scalar", "V", vlast)
            slast = None
            for ch in range(nch):
                o = 512 * ch
                ss(lambda e, o=o: e.activation(
                    out=a32.r(o_stw + 2048 + o, 512),
                    in_=a32.r(o_stw + 1024 + o, 512),
                    func=AF.Ln, bias=EPS, scale=1.0))
                slast = ss(lambda e, o=o: e.activation(
                    out=a32.r(o_stw + 2048 + o, 512),
                    in_=a32.r(o_stw + 2048 + o, 512),
                    func=AF.Exp, bias=0.0, scale=-0.5), True)
            w("vector", "S", slast)
            vlast = None
            for ch in range(nch):
                o = 512 * ch
                vlast = vv(lambda e, o=o: e.tensor_tensor(
                    out=a32.r(o_stw + 3072 + o, 512), in0=a32.r(o_stw + o, 512),
                    in1=a32.r(o_stw + 2048 + o, 512), op=ALU.mult), True)
            w("scalar", "V", vlast)
            slast = None
            for ch in range(nch):
                o = 512 * ch
                ss(lambda e, o=o: e.copy(out=abf.r(o_sbf + o, 512),
                                         in_=a32.r(o_stw + 2048 + o, 512)))
                slast = ss(lambda e, o=o: e.copy(out=abf.r(o_sbf + 1024 + o, 512),
                                                 in_=a32.r(o_stw + 3072 + o, 512)),
                           True)
            w("tensor", "S", slast)
            for ch in range(nch):
                bs, bq = (pE, pF) if ch == 0 else (pG, pH)
                o = 512 * ch
                tt(lambda e, bs=bs, o=o: e.matmul(
                    bs[:, :], onebf[:, :], abf.r(o_sbf + o, 512),
                    start=True, stop=True))
                t1 = tt(lambda e, bq=bq, o=o: e.matmul(
                    bq[:, :], onebf[:, :], abf.r(o_sbf + 1024 + o, 512),
                    start=True, stop=True), True)
            return t1

        def ln_apply(get_x, put, tb, t_bc):
            w("vector", "T", t_bc)
            vlast = None
            for kt in range(8):
                for ch in range(tb // 512):
                    bs, bq = (pE, pF) if ch == 0 else (pG, pH)
                    sl = slice(512 * ch, 512 * (ch + 1))
                    vv(lambda e, kt=kt, sl=sl, bs=bs: e.tensor_tensor(
                        out=put(kt)[:, sl], in0=get_x(kt)[:, sl], in1=bs[:, :],
                        op=ALU.mult))
                    vlast = vv(lambda e, kt=kt, sl=sl, bq=bq: e.tensor_tensor(
                        out=put(kt)[:, sl], in0=put(kt)[:, sl], in1=bq[:, :],
                        op=ALU.subtract), True)
            return vlast

        abf.reset(basebf_wi)
        o_xkt = a32.alloc(2 * TBA)
        o_xpre = a32.alloc(8 * (TBA + 3) + 16)
        o_cacc = a32.alloc(2 * TBA)
        o_xds = a32.alloc(TBA)
        o_xbf = abf.alloc(2 * 8 * TBA)
        o_sq = abf.alloc(1024)
        o_zb = abf.alloc(8 * TBA)
        o_xpb = abf.alloc(8 * TBA)

        def xbuf(tba, kt):
            return abf.v(o_xbf + (tba % 2) * 8 * TBA + kt * TBA, TBA)

        vhalo = None
        for dt in range(8):
            vhalo = vv(lambda e, dt=dt: e.memset(
                a32.v(o_xpre + dt * (TBA + 3), 3), 0.0), True)

        last_xz = {}
        vap_ms = {}
        cast_ms = {}
        tbc_ms = {}
        evs_g = {}
        sxd = None
        txd_ms = {}
        zstore_c = None
        xds_dma = None

        xkt_cast = {}
        xkt_gidx = [0]

        def emit_loads(tba):
            t0 = tba * TBA
            for kt in range(8):
                gi = xkt_gidx[0]
                if kt == 0 and tba >= 2:
                    w("scalar", "T", last_xz[tba - 2])
                    w("scalar", "V", vap_ms[tba - 2])
                if gi >= 2:
                    w("sync", "S", xkt_cast[gi - 2])
                cx = dma("sync", f"dX{gi % 2}",
                         a32.v(o_xkt + (gi % 2) * TBA, TBA),
                         x_f[128 * kt:128 * (kt + 1), t0:t0 + TBA])
                w("scalar", f"dX{gi % 2}", cx)
                xkt_cast[gi] = ss(lambda e, tba=tba, kt=kt,
                                  o=o_xkt + (gi % 2) * TBA: e.copy(
                    out=xbuf(tba, kt), in_=a32.v(o, TBA)), True)
                xkt_gidx[0] += 1
            return xkt_cast[xkt_gidx[0] - 1]

        cast_ms[0] = emit_loads(0)
        tbc_ms[0] = ln_stats(lambda kt: xbuf(0, kt), TBA, o_sq)

        for tba in range(8):
            t0 = tba * TBA
            vap = ln_apply(lambda kt, tba=tba: xbuf(tba, kt),
                           lambda kt, tba=tba: xbuf(tba, kt), TBA, tbc_ms[tba])
            vap_ms[tba] = vap
            if tba + 1 < 8:
                cast_ms[tba + 1] = emit_loads(tba + 1)
            w("scalar", "V", vhalo)
            if zstore_c is not None:
                w("scalar", "dZS", zstore_c)
            for ft in range(16):
                gft = tba * 16 + ft
                bank = pbank[ft % 4]
                if gft >= 4:
                    w("tensor", "S", evs_g[gft - 4])
                tmm = None
                for kt in range(8):
                    if ft == 0 and kt == 0:
                        w("tensor", "V", vap)
                    tmm = tt(lambda e, b=bank, kt=kt, ft=ft, tba=tba: e.matmul(
                        b[:, 0:TBA],
                        wi_bf_sl(2 * DH * kt + 128 * ft,
                                 2 * DH * kt + 128 * (ft + 1)),
                        xbuf(tba, kt),
                        start=(kt == 0), stop=(kt == 7)), True)
                w("scalar", "T", tmm)
                if ft < 8:
                    evs_g[gft] = ss(lambda e, b=bank, ft=ft: e.copy(
                        out=a32.v(o_xpre + ft * (TBA + 3) + 3, TBA),
                        in_=b[:, 0:TBA]), True)
                else:
                    evs_g[gft] = ss(lambda e, b=bank, ft=ft: e.activation(
                        out=abf.v(o_zb + (ft - 8) * TBA, TBA), in_=b[:, 0:TBA],
                        func=AF.Silu), True)
            last_xz[tba] = tmm
            if tba + 1 < 8:
                tbc_ms[tba + 1] = ln_stats(
                    lambda kt, tba=tba: xbuf(tba + 1, kt), TBA, o_sq)
            ssl = None
            sslm = {}
            for dt in range(8):
                ob = o_xpre + dt * (TBA + 3)
                oc = o_cacc + (dt % 2) * TBA
                w("vector", "S", evs_g[tba * 16 + dt])
                if dt >= 2:
                    w("vector", "S", sslm[dt - 2])
                vv(lambda e, dt=dt, ob=ob, oc=oc: e.tensor_scalar(
                    out=a32.v(oc, TBA), in0=a32.v(ob + 0, TBA),
                    scalar1=small[:, cwo + 4 * dt:cwo + 4 * dt + 1],
                    scalar2=None, op0=ALU.mult))
                for j in (1, 2, 3):
                    vv(lambda e, dt=dt, ob=ob, j=j, oc=oc: e.scalar_tensor_tensor(
                        out=a32.v(oc, TBA), in0=a32.v(ob + j, TBA),
                        scalar=small[:, cwo + 4 * dt + j:cwo + 4 * dt + j + 1],
                        in1=a32.v(oc, TBA), op0=ALU.mult, op1=ALU.add))
                vhalo = vv(lambda e, ob=ob: e.tensor_copy(
                    a32.v(ob, 3), a32.v(ob + TBA, 3)), True)
                w("scalar", "V", vhalo)
                if dt == 0 and tba >= 1:
                    w("scalar", "T", txd_ms[tba - 1])
                ssl = ss(lambda e, dt=dt, oc=oc: e.activation(
                    out=abf.v(o_xpb + dt * TBA, TBA), in_=a32.v(oc, TBA),
                    func=AF.Silu, bias=small[:, dt:dt + 1], scale=1.0), True)
                sslm[dt] = ssl
            w("sync", "S", ssl)
            for dt in range(8):
                dma("sync", "dZS", z_d[128 * dt:128 * (dt + 1), t0:t0 + TBA],
                    abf.v(o_zb + dt * TBA, TBA))
                zstore_c = dma("sync", "dZS",
                               xp_d[128 * dt:128 * (dt + 1), t0:t0 + TBA],
                               abf.v(o_xpb + dt * TBA, TBA))
            if sxd is not None:
                w("tensor", "S", sxd)
            txd = None
            for kt in range(8):
                if kt == 0:
                    w("tensor", "S", ssl)
                txd = tt(lambda e, kt=kt: e.matmul(
                    pG[0:96, 0:TBA], wx_bf[:, 96 * kt:96 * (kt + 1)],
                    abf.v(o_xpb + kt * TBA, TBA),
                    start=(kt == 0), stop=(kt == 7)), True)
            txd_ms[tba] = txd
            w("scalar", "T", txd)
            if xds_dma is not None:
                w("scalar", "dXD", xds_dma)
            sxd = ss(lambda e: e.copy(out=af32_t[0:96, o_xds:o_xds + TBA],
                                      in_=pG[0:96, 0:TBA]), True)
            w("sync", "S", sxd)
            xds_dma = dma("sync", "dXD", xdbl_p[:, t0:t0 + TBA],
                          af32_t[0:96, o_xds:o_xds + TBA])

        w("gpsimd", "dXD", p.cnt["dXD"])
        cc1 = p.em("gpsimd", lambda e: e.collective_compute(
            "AllReduce", ALU.add, replica_groups=PAIRS,
            ins=[xdbl_p], outs=[xdbl_s]), "CC", 1)

        if upto == "A":
            if dev >= 1:
                w("sync", "CC", cc1)
                dma("sync", "dDV", dv_xdbl[:, :], xdbl_s[:, :])
                dma("sync", "dDV", dv_xp[:, :], xp_d[:, :])
                dma("sync", "dDV", dv_z[:, :], z_d[:, :])
            for s in DMA_SEMS:
                w("sync", s, p.cnt[s])
            w("sync", "CC", p.cnt["CC"])
            return nc, p

        phase_barrier()
        a32.reset(base32)
        abf.reset(basebf)
        o_dA = a32.alloc(4 * TBB)
        o_yb = a32.alloc(8 * TBB)
        o_eb = a32.alloc(512)
        o_xpB = abf.alloc(8 * TBB)
        o_dz = abf.alloc(8 * TBB)
        o_dl = abf.alloc(8 * TBB)
        o_u = abf.alloc(2 * TBB)
        o_h = abf.alloc(2 * TBB)
        o_Bb = abf.alloc(TBB)
        o_Cb = abf.alloc(TBB)
        o_og = abf.alloc(TBB)
        o_bcr = abf.alloc(4 * TBB)

        o_xdc = o_yb
        vz = vv(lambda e: e.memset(carry[:, :], 0.0), True)
        w("gpsimd", "CC", cc1)
        s_xdbl = None
        for chx in range(8):
            if chx >= 2:
                w("gpsimd", "S", s_xdbl)
            cxs = dma("gpsimd", f"dXC{chx % 2}",
                      af32_t[0:96, o_xdc + (chx % 2) * 512:o_xdc + (chx % 2) * 512 + 512],
                      xdbl_s[:, 512 * chx:512 * (chx + 1)])
            w("scalar", f"dXC{chx % 2}", cxs)
            s_xdbl = ss(lambda e, chx=chx: e.copy(
                out=xdblb[:, 512 * chx:512 * (chx + 1)],
                in_=af32_t[0:96, o_xdc + (chx % 2) * 512:o_xdc + (chx % 2) * 512 + 512]),
                True)
        if dev >= 1:
            w("sync", "CC", cc1)
            dma("sync", "dDV", dv_xdbl[:, :], xdbl_s[:, :])
            dma("sync", "dDV", dv_xp[:, :], xp_d[:, :])
            dma("sync", "dDV", dv_z[:, :], z_d[:, :])

        rs_map = {0: (rs2in, 0), 1: (rs1in, 0), 2: (rs1in, DM), 3: (rs2in, DM)}
        ccr = {}
        ypc = {}
        vg = None
        smo = None
        ccpy_last = None
        for tb in range(4):
            t0 = tb * TBB
            if vg is not None:
                w("sync", "V", vg)
            cxp = []
            for dt in range(8):
                cxp.append(dma("sync", "dXP", abf.v(o_xpB + dt * TBB, TBB),
                               xp_d[128 * dt:128 * (dt + 1), t0:t0 + TBB]))
            dlev = {}
            s_dl = None
            for dt in range(8):
                for ch in range(2):
                    key = dt * 2 + ch
                    bank = pbank[key % 2]
                    if key >= 2:
                        w("tensor", "S", dlev[key - 2])
                    elif tb == 0:
                        w("tensor", "S", s_xdbl)
                    elif smo is not None:
                        w("tensor", "S", smo)
                    tdl = tt(lambda e, b=bank, dt=dt, ch=ch, t0=t0: e.matmul(
                        b[:, :], wdt_bf[:, 128 * dt:128 * (dt + 1)],
                        xdblb[0:DTR, t0 + 512 * ch:t0 + 512 * (ch + 1)],
                        start=True, stop=True), True)
                    w("scalar", "T", tdl)
                    ss(lambda e, b=bank, dt=dt: e.activation(
                        out=a32.v(o_eb, 512), in_=b[:, :], func=AF.Exp,
                        bias=small[:, 8 + dt:8 + dt + 1], scale=1.0))
                    s_dl = ss(lambda e, dt=dt, ch=ch: e.activation(
                        out=abf.v(o_dl + dt * TBB + 512 * ch, 512),
                        in_=a32.v(o_eb, 512), func=AF.Ln, bias=1.0, scale=1.0),
                        True)
                    dlev[key] = s_dl
                    if dev >= 3 and tb == 1 and dt == 0 and ch == 0:
                        s_dl = ss(lambda e: e.copy(out=a32.v(o_dA, 512),
                                                   in_=a32.v(o_eb, 512)), True)
                        s_dl = ss(lambda e, b=bank: e.copy(
                            out=a32.v(o_dA + 512, 512), in_=b[:, :]), True)
                        w("sync", "S", s_dl)
                        dma("sync", "dDV", dv_wdt[:, :], wdt_bf[:, :])
                        cdb = dma("sync", "dDV", dv_fg[:, :],
                                  a32.v(o_dA, TBB))
                        w("scalar", "dDV", cdb)
                        dlev[key] = s_dl
            w("vector", "S", s_dl)
            vdu = None
            w("vector", "dXP", cxp[7])
            for dt in range(8):
                vdu = vv(lambda e, dt=dt: e.tensor_tensor(
                    out=abf.v(o_dz + dt * TBB, TBB),
                    in0=abf.v(o_dl + dt * TBB, TBB),
                    in1=abf.v(o_xpB + dt * TBB, TBB), op=ALU.mult), True)
            pms = {}
            ccpy = {}
            ccpend = []
            sevBC = None
            if ccpy_last is not None:
                w("vector", "S", ccpy_last)
            if tb == 0:
                w("sync", "S", s_xdbl)
            if dev >= 3 and tb == 1:
                w("sync", "V", vdu)
                dma("sync", "dDV", dv_dl[:, :], abf.v(o_dl, 8 * TBB))
                dma("sync", "dDV", dv_cr[:, :], carry[:, :])
                cdd = dma("sync", "dDV", dv_du[:, :], abf.v(o_dz, 8 * TBB))
                w("vector", "dDV", cdd)
            for n in range(NST):
                rb = (n % 2) * TBB
                rc = (2 + n % 2) * TBB
                if n >= 2:
                    w("sync", "V", pms[(n - 2, 7)])
                cb = dma("sync", f"dBC{n % 2}",
                         abf_t[0:1, o_bcr + rb:o_bcr + rb + TBB],
                         xdblb[DTR + n:DTR + n + 1, t0:t0 + TBB])
                cc_ = dma("sync", f"dBC{n % 2}",
                          abf_t[0:1, o_bcr + rc:o_bcr + rc + TBB],
                          xdblb[DTR + NST + n:DTR + NST + n + 1, t0:t0 + TBB])
                if n > 0:
                    w("scalar", "V", pms[(n - 1, 7)])
                for ch in range(2):
                    w("tensor", f"dBC{n % 2}", cc_)
                    if sevBC is not None:
                        w("tensor", "S", sevBC)
                    tt(lambda e, ch=ch, rb=rb: e.matmul(
                        pE[:, :], onebf[:, :],
                        abf_t[0:1, o_bcr + rb + 512 * ch:o_bcr + rb + 512 * (ch + 1)],
                        start=True, stop=True))
                    tb2 = tt(lambda e, ch=ch, rc=rc: e.matmul(
                        pF[:, :], onebf[:, :],
                        abf_t[0:1, o_bcr + rc + 512 * ch:o_bcr + rc + 512 * (ch + 1)],
                        start=True, stop=True), True)
                    w("scalar", "T", tb2)
                    ss(lambda e, ch=ch: e.copy(out=abf.v(o_Bb + 512 * ch, 512),
                                               in_=pE[:, :]))
                    sevBC = ss(lambda e, ch=ch: e.copy(out=abf.v(o_Cb + 512 * ch, 512),
                                                       in_=pF[:, :]), True)
                if dev >= 3 and tb == 1 and n == 1:
                    vdbg2 = vv(lambda e: e.engine_nop(), True)
                    w("sync", "V", vdbg2)
                    cdy = dma("sync", "dDV", dv_y0[:, :], a32.v(o_yb, 8 * TBB))
                    w("vector", "dDV", cdy)
                for dt in range(8):
                    idx = n * 8 + dt
                    gidx = tb * 128 + idx
                    if idx >= 4:
                        w("scalar", "V", pms[((idx - 4) // 8, (idx - 4) % 8)])
                    sdA = ss(lambda e, n=n, dt=dt, idx=idx: e.activation(
                        out=a32.v(o_dA + (idx % 4) * TBB, TBB),
                        in_=abf.v(o_dl + dt * TBB, TBB), func=AF.Exp,
                        scale=small[:, negA + dt * 16 + n:negA + dt * 16 + n + 1]),
                        True)
                    if dt == 0:
                        w("vector", "S", sevBC)
                    if n == 0 and dt == 0:
                        if tb == 0:
                            w("vector", "V", vz)
                        w("vector", "V", vdu)
                    if ypc.get(gidx - 2) is not None:
                        w("vector", "P", ypc[gidx - 2])
                    vv(lambda e, dt=dt, idx=idx: e.tensor_tensor(
                        out=abf.v(o_u + (idx % 2) * TBB, TBB),
                        in0=abf.v(o_dz + dt * TBB, TBB),
                        in1=abf.v(o_Bb, TBB), op=ALU.mult))
                    w("vector", "S", sdA)
                    if ccpy.get(idx - 2) is not None:
                        w("vector", "S", ccpy[idx - 2])
                    vs = vv(lambda e, idx=idx, n=n, dt=dt: e.tensor_tensor_scan(
                        out=abf.v(o_h + (idx % 2) * TBB, TBB),
                        data0=a32.v(o_dA + (idx % 4) * TBB, TBB),
                        data1=abf.v(o_u + (idx % 2) * TBB, TBB),
                        initial=carry[:, n * 8 + dt:n * 8 + dt + 1],
                        op0=ALU.mult, op1=ALU.add), True)
                    ccpend.append((idx, n, dt, vs))
                    if len(ccpend) >= 2:
                        pidx, pn, pdt, pvs = ccpend.pop(0)
                        w("scalar", "V", pvs)
                        ccpy[pidx] = ss(lambda e, n=pn, dt=pdt, idx=pidx: e.copy(
                            out=carry[:, n * 8 + dt:n * 8 + dt + 1],
                            in_=abf_t[:, o_h + (idx % 2) * TBB + TBB - 1:
                                      o_h + (idx % 2) * TBB + TBB]), True)
                    vp = vv(lambda e, idx=idx: e.tensor_tensor(
                        out=abf.v(o_u + (idx % 2) * TBB, TBB),
                        in0=abf.v(o_h + (idx % 2) * TBB, TBB),
                        in1=abf.v(o_Cb, TBB), op=ALU.mult), True)
                    pms[(n, dt)] = vp
                    if dev >= 3 and tb == 1 and n == 0 and dt == 0:
                        vdbg = vv(lambda e: e.engine_nop(), True)
                        w("sync", "V", vdbg)
                        dma("sync", "dDV", dv_bg[:, 0:TBB], abf.v(o_dl, TBB))
                        dma("sync", "dDV", dv_bg[:, TBB:2 * TBB], abf.v(o_dz, TBB))
                        dma("sync", "dDV", dv_bg[:, 2 * TBB:3 * TBB], abf.v(o_Bb, TBB))
                        cdbh = dma("sync", "dDV", dv_bg[:, 3 * TBB:4 * TBB],
                                   abf.v(o_h + (idx % 2) * TBB, TBB))
                        dma("sync", "dDV", dv_bg2[0:96, :], xdblb[:, 0:TBB])
                        w("vector", "dDV", cdbh)
                    w("gpsimd", "V", vp)
                    if n == 0:
                        ypc[gidx] = gg(lambda e, dt=dt, idx=idx: e.tensor_copy(
                            a32.v(o_yb + dt * TBB, TBB),
                            abf.v(o_u + (idx % 2) * TBB, TBB)), True)
                    else:
                        ypc[gidx] = gg(lambda e, dt=dt, idx=idx: e.tensor_tensor(
                            out=a32.v(o_yb + dt * TBB, TBB),
                            in0=a32.v(o_yb + dt * TBB, TBB),
                            in1=abf.v(o_u + (idx % 2) * TBB, TBB), op=ALU.add),
                            True)
            for (pidx, pn, pdt, pvs) in ccpend:
                w("scalar", "V", pvs)
                ccpy[pidx] = ss(lambda e, n=pn, dt=pdt, idx=pidx: e.copy(
                    out=carry[:, n * 8 + dt:n * 8 + dt + 1],
                    in_=abf_t[:, o_h + (idx % 2) * TBB + TBB - 1:
                              o_h + (idx % 2) * TBB + TBB]), True)
            ccpend = []
            ccpy_last = ccpy[127]
            if tb == 3:
                w("gpsimd", "dOG", p.cnt["dOG"])
                ccr[0] = p.em("gpsimd", lambda e: e.collective_compute(
                    "ReduceScatter", ALU.add, replica_groups=PAIRS,
                    ins=[rs1in], outs=[rs1out]), "CC", 1)
            w("sync", "V", pms[(15, 7)])
            w("vector", "P", ypc[tb * 128 + 127])
            cz = []
            for dt in range(8):
                cz.append(dma("sync", "dZL", abf.v(o_dz + dt * TBB, TBB),
                              z_d[128 * dt:128 * (dt + 1), t0:t0 + TBB]))
            vg = None
            for dt in range(8):
                vv(lambda e, dt=dt: e.scalar_tensor_tensor(
                    out=a32.v(o_yb + dt * TBB, TBB),
                    in0=abf.v(o_xpB + dt * TBB, TBB),
                    scalar=small[:, 16 + dt:16 + dt + 1],
                    in1=a32.v(o_yb + dt * TBB, TBB), op0=ALU.mult, op1=ALU.add))
                w("vector", "dZL", cz[7])
                vg = vv(lambda e, dt=dt: e.tensor_tensor(
                    out=abf.v(o_dl + dt * TBB, TBB),
                    in0=a32.v(o_yb + dt * TBB, TBB),
                    in1=abf.v(o_dz + dt * TBB, TBB), op=ALU.mult), True)
            if dev >= 2:
                w("sync", "V", vg)
                for dt in range(8):
                    dma("sync", "dDV", dv_y[128 * dt:128 * (dt + 1), t0:t0 + TBB],
                        a32.v(o_yb + dt * TBB, TBB))
            rs_t, rs_r0 = rs_map[tb]
            opev = {}
            ogdma = None
            for ot in range(8):
                for ch in range(2):
                    key = ot * 2 + ch
                    bank = pbank[key % 2]
                    if key >= 2:
                        w("tensor", "S", opev[key - 2])
                    tmo = None
                    for kt in range(8):
                        if key == 0 and kt == 0:
                            w("tensor", "V", vg)
                        tmo = tt(lambda e, b=bank, ot=ot, ch=ch, kt=kt: e.matmul(
                            b[:, :],
                            wo_bf[:, DM * kt + 128 * ot:DM * kt + 128 * (ot + 1)],
                            abf.v(o_dl + kt * TBB + 512 * ch, 512),
                            start=(kt == 0), stop=(kt == 7)), True)
                    w("scalar", "T", tmo)
                    if ogdma is not None and ch == 0:
                        w("scalar", "dOG", ogdma)
                    smo = ss(lambda e, b=bank, ch=ch: e.copy(
                        out=abf.v(o_og + 512 * ch, 512), in_=b[:, :]), True)
                    opev[key] = smo
                w("sync", "S", smo)
                ogdma = dma("sync", "dOG",
                            rs_t[rs_r0 + 128 * ot:rs_r0 + 128 * (ot + 1), :],
                            abf.v(o_og, TBB))
            if tb == 3:
                w("gpsimd", "dOG", p.cnt["dOG"])
                ccr[1] = p.em("gpsimd", lambda e: e.collective_compute(
                    "ReduceScatter", ALU.add, replica_groups=PAIRS,
                    ins=[rs2in], outs=[rs2out]), "CC", 1)

        if dev >= 2:
            w("sync", "CC", ccr[1])
            dma("sync", "dDV", dv_rs1[:, :], rs1out[:, :])
            dma("sync", "dDV", dv_rs2[:, :], rs2out[:, :])
        if upto == "B":
            for s in DMA_SEMS:
                w("sync", s, p.cnt[s])
            w("sync", "CC", p.cnt["CC"])
            return nc, p

        phase_barrier()
        a32.reset(base32)
        abf.reset(basebf)
        o_x1 = a32.alloc(8 * TBC)
        o_ts = a32.alloc(TBC)
        o_ts2 = a32.alloc(TBC)
        o_zs = a32.alloc(TBC)
        o_x2s = a32.alloc(2 * TBC)
        o_sqC = abf.alloc(1024)
        o_h0 = abf.alloc(8 * TBC)
        o_hf = abf.alloc(8 * TBC)
        o_a = abf.alloc(32 * TBC)
        o_ms = o_a
        o_res = o_a
        o_i1 = abf.alloc(TBC)
        o_m1 = abf.alloc(TBC)
        o_w1b = abf.alloc(4 * 1024)
        o_w2b = abf.alloc(2 * 1024)

        def cdump(src_off, bf=True):
            for kt in range(2):
                vx2 = vv(lambda e, kt=kt, s=src_off, bf=bf: e.tensor_copy(
                    a32.v(o_x2s + (kt % 2) * TBC, TBC),
                    abf.v(s + kt * TBC, TBC) if bf else a32.v(s + kt * TBC, TBC)),
                    True)
                w("sync", "V", vx2)
                dma("sync", "dX2", _x2o_cur[128 * kt:128 * (kt + 1),
                                            _f0_cur:_f0_cur + TBC],
                    a32.v(o_x2s + (kt % 2) * TBC, TBC))

        for rs in range(2):
            rs_out = rs1out if rs == 0 else rs2out
            x_rs = x_rs1 if rs == 0 else x_rs2
            x2o = x2a if rs == 0 else x2b
            w("sync", "CC", ccr[rs])
            for hb in range(2):
                f0 = hb * TBC
                _f0_cur = f0
                _x2o_cur = x2o
                w("sync", "V", p.cnt["V"])
                w("sync", "S", p.cnt["S"])
                cms, cx1 = [], []
                for kt in range(8):
                    cms.append(dma("sync", "dMS", abf.v(o_ms + kt * TBC, TBC),
                                   rs_out[128 * kt:128 * (kt + 1), f0:f0 + TBC]))
                for kt in range(8):
                    cx1.append(dma("sync", "dMS", a32.v(o_x1 + kt * TBC, TBC),
                                   x_rs[128 * kt:128 * (kt + 1), f0:f0 + TBC]))
                w("scalar", "dMS", cx1[-1])
                w("vector", "dMS", cx1[-1])
                w("tensor", "dMS", cx1[-1])
                if cstop == 0:
                    cdump(o_ms)
                    continue
                tbc = ln_stats(lambda kt: abf.v(o_ms + kt * TBC, TBC), TBC, o_sqC)
                vap = ln_apply(lambda kt: abf.v(o_ms + kt * TBC, TBC),
                               lambda kt: abf.v(o_h0 + kt * TBC, TBC), TBC, tbc)
                if cstop == 1:
                    cdump(o_h0)
                    continue
                vx1 = None
                for kt in range(8):
                    if cstop == 22:
                        vx1 = vv(lambda e, kt=kt: e.tensor_tensor(
                            out=a32.v(o_ts, TBC),
                            in0=a32.v(o_x1 + kt * TBC, TBC),
                            in1=abf.v(o_h0 + kt * TBC, TBC), op=ALU.add), True)
                    else:
                        vx1 = vv(lambda e, kt=kt: e.tensor_tensor(
                            out=a32.v(o_x1 + kt * TBC, TBC),
                            in0=a32.v(o_x1 + kt * TBC, TBC),
                            in1=abf.v(o_h0 + kt * TBC, TBC), op=ALU.add), True)
                if cstop in (21, 22):
                    w("sync", "V", vx1)
                    for kt in range(2):
                        vx2 = vv(lambda e, kt=kt: e.tensor_copy(
                            a32.v(o_x2s + (kt % 2) * TBC, TBC), a32.v(o_ts, TBC)),
                            True)
                        w("sync", "V", vx2)
                        dma("sync", "dX2", x2o[128 * kt:128 * (kt + 1), f0:f0 + TBC],
                            a32.v(o_x2s + (kt % 2) * TBC, TBC))
                    continue
                w("scalar", "V", vx1)
                sx1 = None
                for kt in range(8):
                    sx1 = ss(lambda e, kt=kt: e.copy(
                        out=abf.v(o_ms + kt * TBC, TBC),
                        in_=a32.v(o_x1 + kt * TBC, TBC)), True)
                w("vector", "S", sx1)
                w("tensor", "S", sx1)
                if cstop == 23:
                    for kt in range(2):
                        sx2 = ss(lambda e, kt=kt: e.copy(
                            out=a32.v(o_x2s + (kt % 2) * TBC, TBC),
                            in_=abf.v(o_ms + kt * TBC, TBC)), True)
                        w("sync", "S", sx2)
                        dma("sync", "dX2", x2o[128 * kt:128 * (kt + 1), f0:f0 + TBC],
                            a32.v(o_x2s + (kt % 2) * TBC, TBC))
                    continue
                if cstop == 2:
                    cdump(o_ms)
                    continue
                tbc = ln_stats(lambda kt: abf.v(o_ms + kt * TBC, TBC), TBC, o_sqC)
                vap = ln_apply(lambda kt: abf.v(o_ms + kt * TBC, TBC),
                               lambda kt: abf.v(o_h0 + kt * TBC, TBC), TBC, tbc)
                w("scalar", "T", tbc)
                w("scalar", "V", vap)
                ss(lambda e: e.copy(out=abf.v(o_i1, TBC), in_=pE[:, :]))
                si1 = ss(lambda e: e.copy(out=abf.v(o_m1, TBC), in_=pF[:, :]), True)
                w("vector", "S", si1)
                if cstop == 3:
                    cdump(o_h0)
                    continue
                tbc = ln_stats(lambda kt: abf.v(o_h0 + kt * TBC, TBC), TBC, o_sqC)
                vap = ln_apply(lambda kt: abf.v(o_h0 + kt * TBC, TBC),
                               lambda kt: abf.v(o_hf + kt * TBC, TBC), TBC, tbc)
                vhf = None
                for kt in range(8):
                    vhf = vv(lambda e, kt=kt: e.tensor_scalar(
                        out=abf.v(o_hf + kt * TBC, TBC),
                        in0=abf.v(o_hf + kt * TBC, TBC),
                        scalar1=small[:, 24 + kt:24 + kt + 1],
                        scalar2=small[:, 32 + kt:32 + kt + 1],
                        op0=ALU.mult, op1=ALU.add), True)
                if cstop == 4:
                    cdump(o_hf)
                    continue
                tq = {}
                sfev = None
                cw1 = {}
                for i in range(4):
                    q0, k0 = divmod(i, 8)
                    cw1[i] = dma("gpsimd", f"dF{i % 4}",
                                 abf.v(o_w1b + (i % 4) * 1024, 1024),
                                 w1_d[128 * k0:128 * (k0 + 1),
                                      1024 * q0:1024 * (q0 + 1)])
                for q in range(4):
                    if sfev is not None:
                        w("tensor", "S", sfev)
                    tlast = None
                    for kt in range(8):
                        widx = q * 8 + kt
                        w("tensor", f"dF{widx % 4}", cw1[widx])
                        if q == 0 and kt == 0:
                            w("tensor", "V", vhf)
                        for fj in range(8):
                            tlast = tt(lambda e, kt=kt, fj=fj, widx=widx: e.matmul(
                                pbank[fj][:, :],
                                abf_t[:, o_w1b + (widx % 4) * 1024 + 128 * fj:
                                      o_w1b + (widx % 4) * 1024 + 128 * (fj + 1)],
                                abf.v(o_hf + kt * TBC, TBC),
                                start=(kt == 0), stop=(kt == 7)), True)
                        tq[widx] = tlast
                        if widx + 4 < 32:
                            nq, nk = divmod(widx + 4, 8)
                            w("gpsimd", "T", tq[widx])
                            cw1[widx + 4] = dma(
                                "gpsimd", f"dF{widx % 4}",
                                abf.v(o_w1b + (widx % 4) * 1024, 1024),
                                w1_d[128 * nk:128 * (nk + 1),
                                     1024 * nq:1024 * (nq + 1)])
                    w("scalar", "T", tlast)
                    for fj in range(8):
                        ft = q * 8 + fj
                        sfev = ss(lambda e, fj=fj, ft=ft: e.activation(
                            out=abf.v(o_a + ft * TBC, TBC),
                            in_=pbank[fj][:, :], func=AF.Silu,
                            bias=small[:, 40 + ft:40 + ft + 1], scale=1.0), True)
                if cstop == 5:
                    cdump(o_a)
                    continue
                w("tensor", "S", sfev)
                t2q = {}
                t2last = None
                cw2 = {0: dma("gpsimd", "dW20", abf.v(o_w2b, 1024), w2_d[0:128, :]),
                       1: dma("gpsimd", "dW21", abf.v(o_w2b + 1024, 1024),
                              w2_d[128:256, :])}
                for ktf in range(32):
                    w("tensor", f"dW2{ktf % 2}", cw2[ktf])
                    for ot in range(8):
                        t2last = tt(lambda e, ktf=ktf, ot=ot: e.matmul(
                            pbank[ot][:, :],
                            abf_t[:, o_w2b + (ktf % 2) * 1024 + 128 * ot:
                                  o_w2b + (ktf % 2) * 1024 + 128 * (ot + 1)],
                            abf.v(o_a + ktf * TBC, TBC),
                            start=(ktf == 0), stop=(ktf == 31)), True)
                    t2q[ktf] = t2last
                    if ktf + 2 < 32:
                        w("gpsimd", "T", t2q[ktf])
                        cw2[ktf + 2] = dma(
                            "gpsimd", f"dW2{ktf % 2}",
                            abf.v(o_w2b + (ktf % 2) * 1024, 1024),
                            w2_d[128 * (ktf + 2):128 * (ktf + 3), :])
                if cstop == 6:
                    w("scalar", "T", t2last)
                    for kt in range(2):
                        sx2 = ss(lambda e, kt=kt: e.copy(
                            out=a32.v(o_x2s + (kt % 2) * TBC, TBC),
                            in_=pbank[kt][:, :]), True)
                        w("sync", "S", sx2)
                        dma("sync", "dX2", x2o[128 * kt:128 * (kt + 1), f0:f0 + TBC],
                            a32.v(o_x2s + (kt % 2) * TBC, TBC))
                    continue
                w("scalar", "T", t2last)
                vres = None
                for ot in range(8):
                    if vres is not None:
                        w("scalar", "V", vres)
                    szs = ss(lambda e, ot=ot: e.activation(
                        out=a32.v(o_zs, TBC), in_=pbank[ot][:, :], func=AF.Identity,
                        bias=small[:, 72 + ot:72 + ot + 1], scale=0.5), True)
                    vv(lambda e, ot=ot: e.tensor_tensor(
                        out=a32.v(o_ts, TBC), in0=a32.v(o_x1 + ot * TBC, TBC),
                        in1=abf.v(o_i1, TBC), op=ALU.mult))
                    vv(lambda e: e.tensor_tensor(
                        out=a32.v(o_ts2, TBC), in0=a32.v(o_ts, TBC),
                        in1=abf.v(o_m1, TBC), op=ALU.subtract))
                    w("vector", "S", szs)
                    vres = vv(lambda e, ot=ot: e.tensor_tensor(
                        out=abf.v(o_res + ot * TBC, TBC), in0=a32.v(o_ts2, TBC),
                        in1=a32.v(o_zs, TBC), op=ALU.add), True)
                tbc = ln_stats(lambda kt: abf.v(o_res + kt * TBC, TBC), TBC, o_sqC)
                vap = ln_apply(lambda kt: abf.v(o_res + kt * TBC, TBC),
                               lambda kt: abf.v(o_hf + kt * TBC, TBC), TBC, tbc)
                cst = None
                for kt in range(8):
                    if kt >= 2:
                        w("vector", "dX2", cst)
                    vx2 = vv(lambda e, kt=kt: e.tensor_tensor(
                        out=a32.v(o_x2s + (kt % 2) * TBC, TBC),
                        in0=a32.v(o_x1 + kt * TBC, TBC),
                        in1=abf.v(o_hf + kt * TBC, TBC), op=ALU.add), True)
                    w("sync", "V", vx2)
                    cst = dma("sync", "dX2",
                              x2o[128 * kt:128 * (kt + 1), f0:f0 + TBC],
                              a32.v(o_x2s + (kt % 2) * TBC, TBC))

        for s in DMA_SEMS:
            w("sync", s, p.cnt[s])
        w("sync", "CC", p.cnt["CC"])
        return nc, p


def _finish(nc, p):
    with nc.Block() as block:
        @block.sync
        def _(e):
            p.replay("sync", e)

        @block.vector
        def _(e):
            p.replay("vector", e)

        @block.scalar
        def _(e):
            p.replay("scalar", e)

        @block.tensor
        def _(e):
            p.replay("tensor", e)

        @block.gpsimd
        def _(e):
            p.replay("gpsimd", e)
    return nc


_TOK = {
    (0, 0): 1024, (0, 1): 0, (1, 0): 2048, (1, 1): 3072,
}


def make_in_maps(inputs):
    x = np.asarray(inputs["L0_attn_x"])
    Wi = np.asarray(inputs["L0_attn_Wi"])
    cw = np.asarray(inputs["L0_attn_conv_w"])
    cb = np.asarray(inputs["L0_attn_conv_b"])
    Wx = np.asarray(inputs["L0_attn_Wx"])
    Wdt = np.asarray(inputs["L0_attn_Wdt"])
    bdt = np.asarray(inputs["L0_attn_bdt"])
    Alog = np.asarray(inputs["L0_attn_A_log"])
    Dp = np.asarray(inputs["L0_attn_Dp"])
    Wo = np.asarray(inputs["L0_attn_Wo"])
    g = np.asarray(inputs["L0_ffn_ln_gamma"])
    be = np.asarray(inputs["L0_ffn_ln_beta"])
    q1 = np.asarray(inputs["L0_ffn_fc1_q"])
    s1_ = np.asarray(inputs["L0_ffn_fc1_scales"])
    b1 = np.asarray(inputs["L0_ffn_fc1_bias"])
    q2 = np.asarray(inputs["L0_ffn_fc2_q"])
    s2_ = np.asarray(inputs["L0_ffn_fc2_scales"])
    b2 = np.asarray(inputs["L0_ffn_fc2_bias"])

    def u8(q):
        v = q.view(np.uint8).reshape(q.shape[0], q.shape[1], 4)[:, :, 0]
        return np.ascontiguousarray(v)

    def tile8(vec):
        return np.ascontiguousarray(vec.reshape(8, 128).T)

    q1u, q2u = u8(q1), u8(q2)
    in_maps = []
    for c in range(8):
        b, dh = c // 2, c % 2
        sl = slice(dh * DH, (dh + 1) * DH)
        xf = np.ascontiguousarray(x[b].T.astype(np.float32))
        al = Alog[sl].reshape(8, 128, NST).transpose(1, 0, 2).reshape(128, 128)
        d = dict(
            x_f=xf,
            x_rs1=np.ascontiguousarray(xf[:, _TOK[(dh, 0)]:_TOK[(dh, 0)] + TBB]),
            x_rs2=np.ascontiguousarray(xf[:, _TOK[(dh, 1)]:_TOK[(dh, 1)] + TBB]),
            wi_t=np.ascontiguousarray(
                np.concatenate([Wi[sl], Wi[2 * DH + dh * DH:2 * DH + (dh + 1) * DH]],
                               axis=0).T),
            wx_t=np.ascontiguousarray(Wx[:, sl].T),
            wdt_t=np.ascontiguousarray(Wdt[sl].T),
            wo_t=np.ascontiguousarray(Wo[:, sl].T),
            convw=np.ascontiguousarray(cw[sl, 0, :]),
            convb=tile8(cb[sl]),
            bdt_t=tile8(bdt[sl]),
            alog=np.ascontiguousarray(al),
            dp_t=tile8(Dp[sl]),
            gam=tile8(g),
            bet=tile8(be),
            fc1b=np.ascontiguousarray(b1.reshape(32, 128).T),
            fc2b=tile8(b2),
            fc1q=q1u,
            fc2q=q2u,
            fc1s=np.ascontiguousarray(s1_),
            fc2s=np.ascontiguousarray(s2_),
            ones1=np.ones((1, 128), np.float32),
            onesc=np.ones((128, 1), np.float32),
        )
        in_maps.append(d)
    return in_maps


def run_raw(inputs, dev=None, upto=None, cstop=99):
    dev = DEV if dev is None else dev
    upto = BUILD_UPTO if upto is None else upto
    nc, p = build_nc(dev=dev, upto=upto, cstop=cstop)
    _finish(nc, p)
    in_maps = make_in_maps(inputs)
    return run_bass_kernel_spmd(nc, in_maps, core_ids=list(range(8)))


def kernel(**inputs):
    res = run_raw(inputs, dev=0, upto="C")
    out = np.empty((4, L, DM), np.float32)
    for c in range(8):
        b, dh = c // 2, c % 2
        r = res.results[c]
        t1 = _TOK[(dh, 0)]
        t2 = _TOK[(dh, 1)]
        out[b, t1:t1 + TBB, :] = np.asarray(r["x2a"]).T
        out[b, t2:t2 + TBB, :] = np.asarray(r["x2b"]).T
    return out


# version 10
# speedup vs baseline: 1.0158x; 1.0158x over previous
import numpy as np
import concourse.bass as bass
import concourse.mybir as mybir
from concourse.bass_utils import run_bass_kernel_spmd
from contextlib import ExitStack

F32 = mybir.dt.float32
BF16 = mybir.dt.bfloat16
U8 = mybir.dt.uint8
AF = mybir.ActivationFunctionType
ALU = mybir.AluOpType

L = 4096
DM = 1024
DH = 1024
NST = 16
DTR = 64
TBA = 512
TBB = 1024
TBC = 512
EPS = 1e-5
PAIRS = [[0, 1], [2, 3], [4, 5], [6, 7]]

DEV = 0
BUILD_UPTO = "C"

ENGINES = ("sync", "vector", "scalar", "tensor", "gpsimd")


class Plan:
    def __init__(self):
        self.q = {e: [] for e in ENGINES}
        self.cnt = {}
        self.sems = {}

    def add_sem(self, name, handle):
        self.sems[name] = handle
        self.cnt[name] = 0

    def em(self, eng, fn, inc=None, n=1):
        if inc is None:
            self.q[eng].append((fn, None, 0))
            return None
        self.cnt[inc] += n
        self.q[eng].append((fn, inc, n))
        return self.cnt[inc]

    def wait(self, eng, sem, val):
        if val is None or val <= 0:
            return
        self.q[eng].append((None, sem, val))

    def replay(self, eng, e):
        for fn, sem, val in self.q[eng]:
            if fn is None:
                e.wait_ge(self.sems[sem], val)
            else:
                ins = fn(e)
                if sem is not None:
                    ins.then_inc(self.sems[sem], val)


class Arena:
    def __init__(self, t, cols):
        self.t, self.cols, self.off = t, cols, 0

    def alloc(self, n):
        n = (n + 15) // 16 * 16
        assert self.off + n <= self.cols, f"arena overflow {self.off}+{n}>{self.cols}"
        o = self.off
        self.off += n
        return o

    def v(self, o, n):
        return self.t[:, o:o + n]

    def r(self, o, n):
        return self.t[0:1, o:o + n]

    def mark(self):
        return self.off

    def reset(self, m):
        self.off = m


def build_nc(dev=0, upto="C", cstop=99):
    nc = bass.Bass("TRN2", target_bir_lowering=False,
                   detect_race_conditions=False)

    def din(name, shape, dt=F32):
        return nc.dram_tensor(name, shape, dt, kind="ExternalInput").ap()

    def dout(name, shape, dt=F32):
        return nc.dram_tensor(name, shape, dt, kind="ExternalOutput").ap()

    def dint(name, shape, dt):
        return nc.dram_tensor(name, shape, dt).ap()

    x_f = din("x_f", [DM, L])
    x_rs1 = din("x_rs1", [DM, TBB])
    x_rs2 = din("x_rs2", [DM, TBB])
    wi_t = din("wi_t", [DM, 2 * DH])
    wx_t = din("wx_t", [DH, 96])
    wdt_t = din("wdt_t", [DTR, DH])
    wo_t = din("wo_t", [DH, DM])
    convw = din("convw", [DH, 4])
    convb = din("convb", [128, 8])
    bdt_t = din("bdt_t", [128, 8])
    alog = din("alog", [128, 128])
    dp_t = din("dp_t", [128, 8])
    gam = din("gam", [128, 8])
    bet = din("bet", [128, 8])
    fc1b = din("fc1b", [128, 32])
    fc2b = din("fc2b", [128, 8])
    fc1q = din("fc1q", [DM, 4096], U8)
    fc2q = din("fc2q", [4096, DM], U8)
    fc1s = din("fc1s", [8, 4096])
    fc2s = din("fc2s", [32, DM])
    ones1 = din("ones1", [1, 128])
    onesc = din("onesc", [128, 1])

    x2a = dout("x2a", [DM, TBB])
    x2b = dout("x2b", [DM, TBB])
    if dev >= 1:
        dv_xp = dout("dv_xp", [DH, L], BF16)
        dv_z = dout("dv_z", [DH, L], BF16)
        dv_xdbl = dout("dv_xdbl", [96, L])
    if dev >= 2:
        dv_y = dout("dv_y", [DH, L])
        dv_rs1 = dout("dv_rs1", [DM, TBB], BF16)
        dv_rs2 = dout("dv_rs2", [DM, TBB], BF16)
    if dev >= 3:
        dv_dl = dout("dv_dl", [128, 8 * TBB], BF16)
        dv_du = dout("dv_du", [128, 8 * TBB], BF16)
        dv_y0 = dout("dv_y0", [128, 8 * TBB], F32)
        dv_cr = dout("dv_cr", [128, 128], F32)
        dv_bg = dout("dv_bg", [128, 4 * TBB], BF16)
        dv_bg2 = dout("dv_bg2", [96, TBB], BF16)
        dv_fg = dout("dv_fg", [128, TBB], F32)
        dv_wdt = dout("dv_wdt", [DTR, DH], BF16)

    w1_d = dint("w1_d", [DM, 4096], BF16)
    w2_d = dint("w2_d", [4096, DM], BF16)
    z_d = dint("z_d", [DH, L], BF16)
    xp_d = dint("xp_d", [DH, L], BF16)
    xdbl_p = dint("xdbl_p", [96, L], F32)
    xdbl_s = dint("xdbl_s", [96, L], F32)
    rs1in = dint("rs1in", [2 * DM, TBB], BF16)
    rs2in = dint("rs2in", [2 * DM, TBB], BF16)
    rs1out = dint("rs1out", [DM, TBB], BF16)
    rs2out = dint("rs2out", [DM, TBB], BF16)

    p = Plan()
    es = ExitStack()
    with es:
        AF32C = 17600
        ABFC = 40096
        af32_t = es.enter_context(nc.sbuf_tensor("af32", [128, AF32C], F32))
        abf_t = es.enter_context(nc.sbuf_tensor("abf", [128, ABFC], BF16))
        a32 = Arena(af32_t, AF32C)
        abf = Arena(abf_t, ABFC)

        wo_bf = es.enter_context(nc.sbuf_tensor("wo_bf", [128, 8 * DM], BF16))
        wx_bf = es.enter_context(nc.sbuf_tensor("wx_bf", [128, 8 * 96], BF16))
        wdt_f = es.enter_context(nc.sbuf_tensor("wdt_f", [DTR, DH], F32))
        wdt_bf = es.enter_context(nc.sbuf_tensor("wdt_bf", [DTR, DH], BF16))
        small = es.enter_context(nc.sbuf_tensor("small", [128, 256], F32))
        negA = 80
        cwo = 208
        carry = es.enter_context(nc.sbuf_tensor("carry", [128, 128], F32))
        onebf = es.enter_context(nc.sbuf_tensor("onebf", [1, 128], BF16))
        onecbf = es.enter_context(nc.sbuf_tensor("onecbf", [128, 1], BF16))
        alog_sb = es.enter_context(nc.sbuf_tensor("alog_sb", [128, 128], F32))
        qbuf = es.enter_context(nc.sbuf_tensor("qbuf", [128, 4096], U8))
        xdblb = es.enter_context(nc.sbuf_tensor("xdblb", [96, L], BF16))

        pbank = [es.enter_context(nc.psum_tensor(f"pb{i}", [128, 512], F32))
                 for i in range(8)]
        pA, pB, pC_, pD, pE, pF, pG, pH = pbank

        DMA_SEMS = ("dW", "dR0", "dR1", "dQ", "dQS", "dX0", "dX1", "dZS",
                    "dXD", "dDV", "dXC0", "dXC1", "dXP", "dBC0", "dBC1",
                    "dZL", "dOG", "dMS", "dF0", "dF1", "dF2", "dF3",
                    "dW20", "dW21", "dX2")
        for s in DMA_SEMS + ("V", "S", "T", "P", "CC"):
            p.add_sem(s, es.enter_context(nc.semaphore(s)))

        def dma(q, sem, out, in_):
            return p.em(q, lambda e, o=out, i=in_: e.dma_start(out=o, in_=i),
                        inc=sem, n=16)

        def vv(fn, inc=False):
            return p.em("vector", fn, "V" if inc else None)

        def ss(fn, inc=False):
            return p.em("scalar", fn, "S" if inc else None)

        def tt(fn, inc=False):
            return p.em("tensor", fn, "T" if inc else None)

        def gg(fn, inc=False):
            return p.em("gpsimd", fn, "P" if inc else None)

        def w(eng, sem, val):
            p.wait(eng, sem, val)

        cstt = es.enter_context(nc.sbuf_tensor("cstt", [128, 4], F32))
        CONSTS = [-128.0, 1.0, EPS, 0.0]
        for i, val in enumerate(CONSTS):
            nc.const_aps.register(F32, val, cstt[:, i:i + 1])
        vcst = None
        for i, val in enumerate(CONSTS):
            vcst = p.em("vector", lambda e, i=i, val=val, t=cstt: e.memset(
                t[:, i:i + 1], val), "V", 1)
        p.wait("scalar", "V", vcst)
        p.wait("tensor", "V", vcst)

        def phase_barrier():
            for eng in ENGINES:
                for s in ("V", "S", "T", "P"):
                    w(eng, s, p.cnt[s])
                for s in DMA_SEMS:
                    w(eng, s, p.cnt[s])

        o_stw = a32.alloc(4096)
        o_sbf = abf.alloc(2048)
        base32, basebf = a32.mark(), abf.mark()
        o_wi = abf.alloc(8 * 2 * DH)
        basebf_wi = abf.mark()

        def wi_bf_sl(a, b):
            return abf_t[:, o_wi + a:o_wi + b]

        dma("sync", "dW", small[:, 0:8], convb)
        dma("sync", "dW", small[:, 8:16], bdt_t)
        dma("sync", "dW", small[:, 16:24], dp_t)
        dma("sync", "dW", small[:, 24:32], gam)
        dma("sync", "dW", small[:, 32:40], bet)
        dma("sync", "dW", small[:, 40:72], fc1b)
        dma("sync", "dW", small[:, 72:80], fc2b)
        dma("sync", "dW", alog_sb[:, :], alog)
        for dt in range(8):
            dma("sync", "dW", small[:, cwo + 4 * dt:cwo + 4 * (dt + 1)],
                convw[128 * dt:128 * (dt + 1), :])
        dma("sync", "dW", wdt_f[:, :], wdt_t)
        dma("sync", "dW", small[:, 250:251], onesc)
        o_r1 = a32.alloc(128)
        c1 = dma("sync", "dW", a32.r(o_r1, 128), ones1)
        w("scalar", "dW", c1)
        ss(lambda e: e.copy(out=onebf[:, :], in_=a32.r(o_r1, 128)))
        ss(lambda e: e.copy(out=onecbf[:, :], in_=small[:, 250:251]))
        ss(lambda e: e.copy(out=wdt_bf[:, :], in_=wdt_f[:, :]))
        ss(lambda e: e.activation(out=small[:, 72:80], in_=small[:, 72:80],
                                  func=AF.Copy, scale=0.5))
        s1 = ss(lambda e: e.activation(out=small[:, negA:negA + 128],
                                       in_=alog_sb[:, :], func=AF.Exp), True)
        w("vector", "S", s1)
        v1 = vv(lambda e: e.tensor_scalar(
            out=small[:, negA:negA + 128], in0=small[:, negA:negA + 128],
            scalar1=-1.0, scalar2=None, op0=ALU.mult), True)
        w("scalar", "V", v1)

        o_ws = a32.alloc(2 * 2048)
        wjobs = ([(wx_t, kt, 96, lambda kt: wx_bf[:, 96 * kt:96 * (kt + 1)])
                  for kt in range(8)] +
                 [(wo_t, kt, DM, lambda kt: wo_bf[:, DM * kt:DM * (kt + 1)])
                  for kt in range(8)] +
                 [(wi_t, kt, 2 * DH,
                   lambda kt: wi_bf_sl(2 * DH * kt, 2 * DH * (kt + 1)))
                  for kt in range(8)])
        scasts = {}
        for gi, (srcd, kt, cols, dstf) in enumerate(wjobs):
            if gi >= 2:
                w("sync", "S", scasts[gi - 2])
            cw1 = dma("sync", f"dR{gi % 2}", a32.v(o_ws + (gi % 2) * 2048, cols),
                      srcd[128 * kt:128 * (kt + 1), :])
            w("scalar", f"dR{gi % 2}", cw1)
            scasts[gi] = ss(lambda e, kt=kt, cols=cols, dstf=dstf,
                            o=o_ws + (gi % 2) * 2048: e.copy(
                out=dstf(kt), in_=a32.v(o, cols)), True)

        o_scr = a32.alloc(4096)
        o_srb = abf.alloc(4096)
        o_qm = abf.alloc(4096)
        o_wr = abf.alloc(4096)

        def dequant(qdram, sdram, wdram, nkt, ncol, s2=None, sdma=None,
                    vlast=None):
            vprev = {}
            for kt in range(nkt):
                if s2 is not None:
                    w("gpsimd", "S", s2)
                cq = dma("gpsimd", "dQ", qbuf[:, 0:ncol],
                         qdram[128 * kt:128 * (kt + 1), :])
                cs = dma("gpsimd", "dQ", a32.r(o_scr, ncol), sdram[kt:kt + 1, :])
                w("scalar", "dQ", cs)
                if vlast is not None:
                    w("scalar", "V", vlast)
                ss(lambda e, n=ncol: e.copy(out=abf.r(o_srb, n), in_=a32.r(o_scr, n)))
                s2 = ss(lambda e, n=ncol: e.activation(
                    out=abf.v(o_qm, n), in_=qbuf[:, 0:n], func=AF.Identity,
                    bias=-128.0, scale=1.0), True)
                if sdma is not None:
                    w("vector", "dQS", sdma)
                for ch in range(ncol // 512):
                    bank = pbank[ch % 2]
                    if (kt, ch) != (0, 0):
                        pv = vprev.get((kt, ch - 2)) or vprev.get(
                            (kt - 1, ncol // 512 - 2 + ch))
                        w("tensor", "V", pv)
                    if ch == 0:
                        w("tensor", "S", s2)
                    t1 = tt(lambda e, b=bank, ch=ch: e.matmul(
                        b[:, :], onebf[:, :], abf.r(o_srb + 512 * ch, 512),
                        start=True, stop=True), True)
                    w("vector", "T", t1)
                    w("vector", "S", s2)
                    vlast = vv(lambda e, b=bank, ch=ch: e.tensor_tensor(
                        out=abf.v(o_wr + 512 * ch, 512),
                        in0=abf.v(o_qm + 512 * ch, 512), in1=b[:, :],
                        op=ALU.mult), True)
                    vprev[(kt, ch)] = vlast
                w("sync", "V", vlast)
                sdma = dma("sync", "dQS", wdram[128 * kt:128 * (kt + 1), :],
                           abf.v(o_wr, ncol))
            return s2, sdma, vlast

        dq_state = dequant(fc1q, fc1s, w1_d, 8, 4096)
        dequant(fc2q, fc2s, w2_d, 32, DM, *dq_state)
        phase_barrier()
        a32.reset(base32)
        abf.reset(basebf)

        def ln_stats(get_x, tb, o_sq, D=1024):
            nch = tb // 512
            w("tensor", "V", p.cnt["V"])
            w("tensor", "S", p.cnt["S"])
            t1 = None
            for ch in range(nch):
                bs, bq = (pE, pF) if ch == 0 else (pG, pH)
                tq_prev = None
                for kt in range(8):
                    if tq_prev is not None:
                        w("scalar", "T", tq_prev)
                    s1 = ss(lambda e, kt=kt, ch=ch: e.activation(
                        out=abf.v(o_sq + (kt % 2) * 512, 512),
                        in_=get_x(kt)[:, 512 * ch:512 * (ch + 1)],
                        func=AF.Square), True)
                    w("tensor", "S", s1)
                    tt(lambda e, kt=kt, ch=ch, bs=bs: e.matmul(
                        bs[0:1, :], onecbf[:, :],
                        get_x(kt)[:, 512 * ch:512 * (ch + 1)],
                        start=(kt == 0), stop=(kt == 7)))
                    t1 = tt(lambda e, kt=kt, bq=bq: e.matmul(
                        bq[0:1, :], onecbf[:, :], abf.v(o_sq + (kt % 2) * 512, 512),
                        start=(kt == 0), stop=(kt == 7)), True)
                    if kt >= 1:
                        tq_prev = t1
            w("vector", "T", t1)
            vlast = None
            for ch in range(nch):
                bs, bq = (pE, pF) if ch == 0 else (pG, pH)
                o = 512 * ch
                vv(lambda e, bs=bs, o=o: e.tensor_scalar(
                    out=a32.r(o_stw + o, 512), in0=bs[0:1, :], scalar1=1.0 / D,
                    scalar2=None, op0=ALU.mult))
                vv(lambda e, bq=bq, o=o: e.tensor_scalar(
                    out=a32.r(o_stw + 1024 + o, 512), in0=bq[0:1, :],
                    scalar1=1.0 / D, scalar2=None, op0=ALU.mult))
                vv(lambda e, o=o: e.tensor_tensor(
                    out=a32.r(o_stw + 2048 + o, 512), in0=a32.r(o_stw + o, 512),
                    in1=a32.r(o_stw + o, 512), op=ALU.mult))
                vlast = vv(lambda e, o=o: e.tensor_tensor(
                    out=a32.r(o_stw + 1024 + o, 512),
                    in0=a32.r(o_stw + 1024 + o, 512),
                    in1=a32.r(o_stw + 2048 + o, 512), op=ALU.subtract), True)
            w("scalar", "V", vlast)
            slast = None
            for ch in range(nch):
                o = 512 * ch
                ss(lambda e, o=o: e.activation(
                    out=a32.r(o_stw + 2048 + o, 512),
                    in_=a32.r(o_stw + 1024 + o, 512),
                    func=AF.Ln, bias=EPS, scale=1.0))
                slast = ss(lambda e, o=o: e.activation(
                    out=a32.r(o_stw + 2048 + o, 512),
                    in_=a32.r(o_stw + 2048 + o, 512),
                    func=AF.Exp, bias=0.0, scale=-0.5), True)
            w("vector", "S", slast)
            vlast = None
            for ch in range(nch):
                o = 512 * ch
                vlast = vv(lambda e, o=o: e.tensor_tensor(
                    out=a32.r(o_stw + 3072 + o, 512), in0=a32.r(o_stw + o, 512),
                    in1=a32.r(o_stw + 2048 + o, 512), op=ALU.mult), True)
            w("scalar", "V", vlast)
            slast = None
            for ch in range(nch):
                o = 512 * ch
                ss(lambda e, o=o: e.copy(out=abf.r(o_sbf + o, 512),
                                         in_=a32.r(o_stw + 2048 + o, 512)))
                slast = ss(lambda e, o=o: e.copy(out=abf.r(o_sbf + 1024 + o, 512),
                                                 in_=a32.r(o_stw + 3072 + o, 512)),
                           True)
            w("tensor", "S", slast)
            for ch in range(nch):
                bs, bq = (pE, pF) if ch == 0 else (pG, pH)
                o = 512 * ch
                tt(lambda e, bs=bs, o=o: e.matmul(
                    bs[:, :], onebf[:, :], abf.r(o_sbf + o, 512),
                    start=True, stop=True))
                t1 = tt(lambda e, bq=bq, o=o: e.matmul(
                    bq[:, :], onebf[:, :], abf.r(o_sbf + 1024 + o, 512),
                    start=True, stop=True), True)
            return t1

        def ln_apply(get_x, put, tb, t_bc):
            w("vector", "T", t_bc)
            vlast = None
            for kt in range(8):
                for ch in range(tb // 512):
                    bs, bq = (pE, pF) if ch == 0 else (pG, pH)
                    sl = slice(512 * ch, 512 * (ch + 1))
                    vv(lambda e, kt=kt, sl=sl, bs=bs: e.tensor_tensor(
                        out=put(kt)[:, sl], in0=get_x(kt)[:, sl], in1=bs[:, :],
                        op=ALU.mult))
                    vlast = vv(lambda e, kt=kt, sl=sl, bq=bq: e.tensor_tensor(
                        out=put(kt)[:, sl], in0=put(kt)[:, sl], in1=bq[:, :],
                        op=ALU.subtract), True)
            return vlast

        abf.reset(basebf_wi)
        o_xkt = a32.alloc(2 * TBA)
        o_xpre = a32.alloc(8 * (TBA + 3) + 16)
        o_cacc = a32.alloc(2 * TBA)
        o_xds = a32.alloc(TBA)
        o_xbf = abf.alloc(2 * 8 * TBA)
        o_sq = abf.alloc(1024)
        o_zb = abf.alloc(8 * TBA)
        o_xpb = abf.alloc(8 * TBA)

        def xbuf(tba, kt):
            return abf.v(o_xbf + (tba % 2) * 8 * TBA + kt * TBA, TBA)

        vhalo = None
        for dt in range(8):
            vhalo = vv(lambda e, dt=dt: e.memset(
                a32.v(o_xpre + dt * (TBA + 3), 3), 0.0), True)

        last_xz = {}
        vap_ms = {}
        cast_ms = {}
        tbc_ms = {}
        evs_g = {}
        sxd = None
        txd_ms = {}
        zstore_c = None
        xds_dma = None

        xkt_cast = {}
        xkt_gidx = [0]

        def emit_loads(tba):
            t0 = tba * TBA
            for kt in range(8):
                gi = xkt_gidx[0]
                if kt == 0 and tba >= 2:
                    w("scalar", "T", last_xz[tba - 2])
                    w("scalar", "V", vap_ms[tba - 2])
                if gi >= 2:
                    w("sync", "S", xkt_cast[gi - 2])
                cx = dma("sync", f"dX{gi % 2}",
                         a32.v(o_xkt + (gi % 2) * TBA, TBA),
                         x_f[128 * kt:128 * (kt + 1), t0:t0 + TBA])
                w("scalar", f"dX{gi % 2}", cx)
                xkt_cast[gi] = ss(lambda e, tba=tba, kt=kt,
                                  o=o_xkt + (gi % 2) * TBA: e.copy(
                    out=xbuf(tba, kt), in_=a32.v(o, TBA)), True)
                xkt_gidx[0] += 1
            return xkt_cast[xkt_gidx[0] - 1]

        cast_ms[0] = emit_loads(0)
        tbc_ms[0] = ln_stats(lambda kt: xbuf(0, kt), TBA, o_sq)

        for tba in range(8):
            t0 = tba * TBA
            vap = ln_apply(lambda kt, tba=tba: xbuf(tba, kt),
                           lambda kt, tba=tba: xbuf(tba, kt), TBA, tbc_ms[tba])
            vap_ms[tba] = vap
            if tba + 1 < 8:
                cast_ms[tba + 1] = emit_loads(tba + 1)
            w("scalar", "V", vhalo)
            if zstore_c is not None:
                w("scalar", "dZS", zstore_c)
            for ft in range(16):
                gft = tba * 16 + ft
                bank = pbank[ft % 4]
                if gft >= 4:
                    w("tensor", "S", evs_g[gft - 4])
                tmm = None
                for kt in range(8):
                    if ft == 0 and kt == 0:
                        w("tensor", "V", vap)
                    tmm = tt(lambda e, b=bank, kt=kt, ft=ft, tba=tba: e.matmul(
                        b[:, 0:TBA],
                        wi_bf_sl(2 * DH * kt + 128 * ft,
                                 2 * DH * kt + 128 * (ft + 1)),
                        xbuf(tba, kt),
                        start=(kt == 0), stop=(kt == 7)), True)
                w("scalar", "T", tmm)
                if ft < 8:
                    evs_g[gft] = ss(lambda e, b=bank, ft=ft: e.copy(
                        out=a32.v(o_xpre + ft * (TBA + 3) + 3, TBA),
                        in_=b[:, 0:TBA]), True)
                else:
                    evs_g[gft] = ss(lambda e, b=bank, ft=ft: e.activation(
                        out=abf.v(o_zb + (ft - 8) * TBA, TBA), in_=b[:, 0:TBA],
                        func=AF.Silu), True)
            last_xz[tba] = tmm
            if tba + 1 < 8:
                tbc_ms[tba + 1] = ln_stats(
                    lambda kt, tba=tba: xbuf(tba + 1, kt), TBA, o_sq)
            ssl = None
            sslm = {}
            for dt in range(8):
                ob = o_xpre + dt * (TBA + 3)
                oc = o_cacc + (dt % 2) * TBA
                w("vector", "S", evs_g[tba * 16 + dt])
                if dt >= 2:
                    w("vector", "S", sslm[dt - 2])
                vv(lambda e, dt=dt, ob=ob, oc=oc: e.tensor_scalar(
                    out=a32.v(oc, TBA), in0=a32.v(ob + 0, TBA),
                    scalar1=small[:, cwo + 4 * dt:cwo + 4 * dt + 1],
                    scalar2=None, op0=ALU.mult))
                for j in (1, 2, 3):
                    vv(lambda e, dt=dt, ob=ob, j=j, oc=oc: e.scalar_tensor_tensor(
                        out=a32.v(oc, TBA), in0=a32.v(ob + j, TBA),
                        scalar=small[:, cwo + 4 * dt + j:cwo + 4 * dt + j + 1],
                        in1=a32.v(oc, TBA), op0=ALU.mult, op1=ALU.add))
                vhalo = vv(lambda e, ob=ob: e.tensor_copy(
                    a32.v(ob, 3), a32.v(ob + TBA, 3)), True)
                w("scalar", "V", vhalo)
                if dt == 0 and tba >= 1:
                    w("scalar", "T", txd_ms[tba - 1])
                ssl = ss(lambda e, dt=dt, oc=oc: e.activation(
                    out=abf.v(o_xpb + dt * TBA, TBA), in_=a32.v(oc, TBA),
                    func=AF.Silu, bias=small[:, dt:dt + 1], scale=1.0), True)
                sslm[dt] = ssl
            w("sync", "S", ssl)
            for dt in range(8):
                dma("sync", "dZS", z_d[128 * dt:128 * (dt + 1), t0:t0 + TBA],
                    abf.v(o_zb + dt * TBA, TBA))
                zstore_c = dma("sync", "dZS",
                               xp_d[128 * dt:128 * (dt + 1), t0:t0 + TBA],
                               abf.v(o_xpb + dt * TBA, TBA))
            if sxd is not None:
                w("tensor", "S", sxd)
            txd = None
            for kt in range(8):
                if kt == 0:
                    w("tensor", "S", ssl)
                txd = tt(lambda e, kt=kt: e.matmul(
                    pG[0:96, 0:TBA], wx_bf[:, 96 * kt:96 * (kt + 1)],
                    abf.v(o_xpb + kt * TBA, TBA),
                    start=(kt == 0), stop=(kt == 7)), True)
            txd_ms[tba] = txd
            w("scalar", "T", txd)
            if xds_dma is not None:
                w("scalar", "dXD", xds_dma)
            sxd = ss(lambda e: e.copy(out=af32_t[0:96, o_xds:o_xds + TBA],
                                      in_=pG[0:96, 0:TBA]), True)
            w("sync", "S", sxd)
            xds_dma = dma("sync", "dXD", xdbl_p[:, t0:t0 + TBA],
                          af32_t[0:96, o_xds:o_xds + TBA])

        w("gpsimd", "dXD", p.cnt["dXD"])
        cc1 = p.em("gpsimd", lambda e: e.collective_compute(
            "AllReduce", ALU.add, replica_groups=PAIRS,
            ins=[xdbl_p], outs=[xdbl_s]), "CC", 1)

        if upto == "A":
            if dev >= 1:
                w("sync", "CC", cc1)
                dma("sync", "dDV", dv_xdbl[:, :], xdbl_s[:, :])
                dma("sync", "dDV", dv_xp[:, :], xp_d[:, :])
                dma("sync", "dDV", dv_z[:, :], z_d[:, :])
            for s in DMA_SEMS:
                w("sync", s, p.cnt[s])
            w("sync", "CC", p.cnt["CC"])
            return nc, p

        phase_barrier()
        a32.reset(base32)
        abf.reset(basebf)
        o_dA = a32.alloc(4 * TBB)
        o_yb = a32.alloc(8 * TBB)
        o_eb = a32.alloc(512)
        o_xpB = abf.alloc(8 * TBB)
        o_dz = abf.alloc(8 * TBB)
        o_dl = abf.alloc(8 * TBB)
        o_u = abf.alloc(4 * TBB)
        o_h = abf.alloc(2 * TBB)
        o_Bb = abf.alloc(TBB)
        o_Cb = abf.alloc(TBB)
        o_og = abf.alloc(TBB)
        o_bcr = abf.alloc(4 * TBB)

        o_xdc = o_yb
        vz = vv(lambda e: e.memset(carry[:, :], 0.0), True)
        w("gpsimd", "CC", cc1)
        s_xdbl = None
        for chx in range(8):
            if chx >= 2:
                w("gpsimd", "S", s_xdbl)
            cxs = dma("gpsimd", f"dXC{chx % 2}",
                      af32_t[0:96, o_xdc + (chx % 2) * 512:o_xdc + (chx % 2) * 512 + 512],
                      xdbl_s[:, 512 * chx:512 * (chx + 1)])
            w("scalar", f"dXC{chx % 2}", cxs)
            s_xdbl = ss(lambda e, chx=chx: e.copy(
                out=xdblb[:, 512 * chx:512 * (chx + 1)],
                in_=af32_t[0:96, o_xdc + (chx % 2) * 512:o_xdc + (chx % 2) * 512 + 512]),
                True)
        if dev >= 1:
            w("sync", "CC", cc1)
            dma("sync", "dDV", dv_xdbl[:, :], xdbl_s[:, :])
            dma("sync", "dDV", dv_xp[:, :], xp_d[:, :])
            dma("sync", "dDV", dv_z[:, :], z_d[:, :])

        rs_map = {0: (rs2in, 0), 1: (rs1in, 0), 2: (rs1in, DM), 3: (rs2in, DM)}
        ccr = {}
        ypc = {}
        vg = None
        smo = None
        ccpy_last = None
        for tb in range(4):
            t0 = tb * TBB
            if vg is not None:
                w("sync", "V", vg)
            cxp = []
            for dt in range(8):
                cxp.append(dma("sync", "dXP", abf.v(o_xpB + dt * TBB, TBB),
                               xp_d[128 * dt:128 * (dt + 1), t0:t0 + TBB]))
            dlev = {}
            s_dl = None
            for dt in range(8):
                for ch in range(2):
                    key = dt * 2 + ch
                    bank = pbank[key % 2]
                    if key >= 2:
                        w("tensor", "S", dlev[key - 2])
                    elif tb == 0:
                        w("tensor", "S", s_xdbl)
                    elif smo is not None:
                        w("tensor", "S", smo)
                    tdl = tt(lambda e, b=bank, dt=dt, ch=ch, t0=t0: e.matmul(
                        b[:, :], wdt_bf[:, 128 * dt:128 * (dt + 1)],
                        xdblb[0:DTR, t0 + 512 * ch:t0 + 512 * (ch + 1)],
                        start=True, stop=True), True)
                    w("scalar", "T", tdl)
                    ss(lambda e, b=bank, dt=dt: e.activation(
                        out=a32.v(o_eb, 512), in_=b[:, :], func=AF.Exp,
                        bias=small[:, 8 + dt:8 + dt + 1], scale=1.0))
                    s_dl = ss(lambda e, dt=dt, ch=ch: e.activation(
                        out=abf.v(o_dl + dt * TBB + 512 * ch, 512),
                        in_=a32.v(o_eb, 512), func=AF.Ln, bias=1.0, scale=1.0),
                        True)
                    dlev[key] = s_dl
                    if dev >= 3 and tb == 1 and dt == 0 and ch == 0:
                        s_dl = ss(lambda e: e.copy(out=a32.v(o_dA, 512),
                                                   in_=a32.v(o_eb, 512)), True)
                        s_dl = ss(lambda e, b=bank: e.copy(
                            out=a32.v(o_dA + 512, 512), in_=b[:, :]), True)
                        w("sync", "S", s_dl)
                        dma("sync", "dDV", dv_wdt[:, :], wdt_bf[:, :])
                        cdb = dma("sync", "dDV", dv_fg[:, :],
                                  a32.v(o_dA, TBB))
                        w("scalar", "dDV", cdb)
                        dlev[key] = s_dl
            w("vector", "S", s_dl)
            vdu = None
            w("vector", "dXP", cxp[7])
            for dt in range(8):
                vdu = vv(lambda e, dt=dt: e.tensor_tensor(
                    out=abf.v(o_dz + dt * TBB, TBB),
                    in0=abf.v(o_dl + dt * TBB, TBB),
                    in1=abf.v(o_xpB + dt * TBB, TBB), op=ALU.mult), True)
            pms = {}
            ccpy = {}
            ccpend = []
            sevBC = None
            if ccpy_last is not None:
                w("vector", "S", ccpy_last)
            if tb == 0:
                w("sync", "S", s_xdbl)
            if dev >= 3 and tb == 1:
                w("sync", "V", vdu)
                dma("sync", "dDV", dv_dl[:, :], abf.v(o_dl, 8 * TBB))
                dma("sync", "dDV", dv_cr[:, :], carry[:, :])
                cdd = dma("sync", "dDV", dv_du[:, :], abf.v(o_dz, 8 * TBB))
                w("vector", "dDV", cdd)
            for n in range(NST):
                rb = (n % 2) * TBB
                rc = (2 + n % 2) * TBB
                if n >= 2:
                    w("sync", "V", pms[(n - 2, 7)])
                cb = dma("sync", f"dBC{n % 2}",
                         abf_t[0:1, o_bcr + rb:o_bcr + rb + TBB],
                         xdblb[DTR + n:DTR + n + 1, t0:t0 + TBB])
                cc_ = dma("sync", f"dBC{n % 2}",
                          abf_t[0:1, o_bcr + rc:o_bcr + rc + TBB],
                          xdblb[DTR + NST + n:DTR + NST + n + 1, t0:t0 + TBB])
                if n > 0:
                    w("scalar", "V", pms[(n - 1, 7)])
                for ch in range(2):
                    w("tensor", f"dBC{n % 2}", cc_)
                    if sevBC is not None:
                        w("tensor", "S", sevBC)
                    tt(lambda e, ch=ch, rb=rb: e.matmul(
                        pE[:, :], onebf[:, :],
                        abf_t[0:1, o_bcr + rb + 512 * ch:o_bcr + rb + 512 * (ch + 1)],
                        start=True, stop=True))
                    tb2 = tt(lambda e, ch=ch, rc=rc: e.matmul(
                        pF[:, :], onebf[:, :],
                        abf_t[0:1, o_bcr + rc + 512 * ch:o_bcr + rc + 512 * (ch + 1)],
                        start=True, stop=True), True)
                    w("scalar", "T", tb2)
                    ss(lambda e, ch=ch: e.copy(out=abf.v(o_Bb + 512 * ch, 512),
                                               in_=pE[:, :]))
                    sevBC = ss(lambda e, ch=ch: e.copy(out=abf.v(o_Cb + 512 * ch, 512),
                                                       in_=pF[:, :]), True)
                if dev >= 3 and tb == 1 and n == 1:
                    vdbg2 = vv(lambda e: e.engine_nop(), True)
                    w("sync", "V", vdbg2)
                    cdy = dma("sync", "dDV", dv_y0[:, :], a32.v(o_yb, 8 * TBB))
                    w("vector", "dDV", cdy)
                for dt in range(8):
                    idx = n * 8 + dt
                    gidx = tb * 128 + idx
                    if idx >= 4:
                        w("scalar", "V", pms[((idx - 4) // 8, (idx - 4) % 8)])
                    sdA = ss(lambda e, n=n, dt=dt, idx=idx: e.activation(
                        out=a32.v(o_dA + (idx % 4) * TBB, TBB),
                        in_=abf.v(o_dl + dt * TBB, TBB), func=AF.Exp,
                        scale=small[:, negA + dt * 16 + n:negA + dt * 16 + n + 1]),
                        True)
                    if dt == 0:
                        w("vector", "S", sevBC)
                    if n == 0 and dt == 0:
                        if tb == 0:
                            w("vector", "V", vz)
                        w("vector", "V", vdu)
                    if ypc.get(gidx - 4) is not None:
                        w("vector", "P", ypc[gidx - 4])
                    vv(lambda e, dt=dt, idx=idx: e.tensor_tensor(
                        out=abf.v(o_u + (idx % 4) * TBB, TBB),
                        in0=abf.v(o_dz + dt * TBB, TBB),
                        in1=abf.v(o_Bb, TBB), op=ALU.mult))
                    w("vector", "S", sdA)
                    if ccpy.get(idx - 2) is not None:
                        w("vector", "S", ccpy[idx - 2])
                    vs = vv(lambda e, idx=idx, n=n, dt=dt: e.tensor_tensor_scan(
                        out=abf.v(o_h + (idx % 2) * TBB, TBB),
                        data0=a32.v(o_dA + (idx % 4) * TBB, TBB),
                        data1=abf.v(o_u + (idx % 4) * TBB, TBB),
                        initial=carry[:, n * 8 + dt:n * 8 + dt + 1],
                        op0=ALU.mult, op1=ALU.add), True)
                    ccpend.append((idx, n, dt, vs))
                    if len(ccpend) >= 2:
                        pidx, pn, pdt, pvs = ccpend.pop(0)
                        w("scalar", "V", pvs)
                        ccpy[pidx] = ss(lambda e, n=pn, dt=pdt, idx=pidx: e.copy(
                            out=carry[:, n * 8 + dt:n * 8 + dt + 1],
                            in_=abf_t[:, o_h + (idx % 2) * TBB + TBB - 1:
                                      o_h + (idx % 2) * TBB + TBB]), True)
                    vp = vv(lambda e, idx=idx: e.tensor_tensor(
                        out=abf.v(o_u + (idx % 4) * TBB, TBB),
                        in0=abf.v(o_h + (idx % 2) * TBB, TBB),
                        in1=abf.v(o_Cb, TBB), op=ALU.mult), True)
                    pms[(n, dt)] = vp
                    if dev >= 3 and tb == 1 and n == 0 and dt == 0:
                        vdbg = vv(lambda e: e.engine_nop(), True)
                        w("sync", "V", vdbg)
                        dma("sync", "dDV", dv_bg[:, 0:TBB], abf.v(o_dl, TBB))
                        dma("sync", "dDV", dv_bg[:, TBB:2 * TBB], abf.v(o_dz, TBB))
                        dma("sync", "dDV", dv_bg[:, 2 * TBB:3 * TBB], abf.v(o_Bb, TBB))
                        cdbh = dma("sync", "dDV", dv_bg[:, 3 * TBB:4 * TBB],
                                   abf.v(o_h + (idx % 2) * TBB, TBB))
                        dma("sync", "dDV", dv_bg2[0:96, :], xdblb[:, 0:TBB])
                        w("vector", "dDV", cdbh)
                    if idx % 2 == 1:
                        w("gpsimd", "V", vp)
                        for bi in (idx - 1, idx):
                            bn, bdt = bi // 8, bi % 8
                            if bn == 0:
                                ypc[tb * 128 + bi] = gg(
                                    lambda e, dt=bdt, idx=bi: e.tensor_copy(
                                        a32.v(o_yb + dt * TBB, TBB),
                                        abf.v(o_u + (idx % 4) * TBB, TBB)), True)
                            else:
                                ypc[tb * 128 + bi] = gg(
                                    lambda e, dt=bdt, idx=bi: e.tensor_tensor(
                                        out=a32.v(o_yb + dt * TBB, TBB),
                                        in0=a32.v(o_yb + dt * TBB, TBB),
                                        in1=abf.v(o_u + (idx % 4) * TBB, TBB),
                                        op=ALU.add), True)
            for (pidx, pn, pdt, pvs) in ccpend:
                w("scalar", "V", pvs)
                ccpy[pidx] = ss(lambda e, n=pn, dt=pdt, idx=pidx: e.copy(
                    out=carry[:, n * 8 + dt:n * 8 + dt + 1],
                    in_=abf_t[:, o_h + (idx % 2) * TBB + TBB - 1:
                              o_h + (idx % 2) * TBB + TBB]), True)
            ccpend = []
            ccpy_last = ccpy[127]
            if tb == 3:
                w("gpsimd", "dOG", p.cnt["dOG"])
                ccr[0] = p.em("gpsimd", lambda e: e.collective_compute(
                    "ReduceScatter", ALU.add, replica_groups=PAIRS,
                    ins=[rs1in], outs=[rs1out]), "CC", 1)
            w("sync", "V", pms[(15, 7)])
            w("vector", "P", ypc[tb * 128 + 127])
            cz = []
            for dt in range(8):
                cz.append(dma("sync", "dZL", abf.v(o_dz + dt * TBB, TBB),
                              z_d[128 * dt:128 * (dt + 1), t0:t0 + TBB]))
            vg = None
            for dt in range(8):
                vv(lambda e, dt=dt: e.scalar_tensor_tensor(
                    out=a32.v(o_yb + dt * TBB, TBB),
                    in0=abf.v(o_xpB + dt * TBB, TBB),
                    scalar=small[:, 16 + dt:16 + dt + 1],
                    in1=a32.v(o_yb + dt * TBB, TBB), op0=ALU.mult, op1=ALU.add))
                w("vector", "dZL", cz[7])
                vg = vv(lambda e, dt=dt: e.tensor_tensor(
                    out=abf.v(o_dl + dt * TBB, TBB),
                    in0=a32.v(o_yb + dt * TBB, TBB),
                    in1=abf.v(o_dz + dt * TBB, TBB), op=ALU.mult), True)
            if dev >= 2:
                w("sync", "V", vg)
                for dt in range(8):
                    dma("sync", "dDV", dv_y[128 * dt:128 * (dt + 1), t0:t0 + TBB],
                        a32.v(o_yb + dt * TBB, TBB))
            rs_t, rs_r0 = rs_map[tb]
            opev = {}
            ogdma = None
            for ot in range(8):
                for ch in range(2):
                    key = ot * 2 + ch
                    bank = pbank[key % 2]
                    if key >= 2:
                        w("tensor", "S", opev[key - 2])
                    tmo = None
                    for kt in range(8):
                        if key == 0 and kt == 0:
                            w("tensor", "V", vg)
                        tmo = tt(lambda e, b=bank, ot=ot, ch=ch, kt=kt: e.matmul(
                            b[:, :],
                            wo_bf[:, DM * kt + 128 * ot:DM * kt + 128 * (ot + 1)],
                            abf.v(o_dl + kt * TBB + 512 * ch, 512),
                            start=(kt == 0), stop=(kt == 7)), True)
                    w("scalar", "T", tmo)
                    if ogdma is not None and ch == 0:
                        w("scalar", "dOG", ogdma)
                    smo = ss(lambda e, b=bank, ch=ch: e.copy(
                        out=abf.v(o_og + 512 * ch, 512), in_=b[:, :]), True)
                    opev[key] = smo
                w("sync", "S", smo)
                ogdma = dma("sync", "dOG",
                            rs_t[rs_r0 + 128 * ot:rs_r0 + 128 * (ot + 1), :],
                            abf.v(o_og, TBB))
            if tb == 3:
                w("gpsimd", "dOG", p.cnt["dOG"])
                ccr[1] = p.em("gpsimd", lambda e: e.collective_compute(
                    "ReduceScatter", ALU.add, replica_groups=PAIRS,
                    ins=[rs2in], outs=[rs2out]), "CC", 1)

        if dev >= 2:
            w("sync", "CC", ccr[1])
            dma("sync", "dDV", dv_rs1[:, :], rs1out[:, :])
            dma("sync", "dDV", dv_rs2[:, :], rs2out[:, :])
        if upto == "B":
            for s in DMA_SEMS:
                w("sync", s, p.cnt[s])
            w("sync", "CC", p.cnt["CC"])
            return nc, p

        phase_barrier()
        a32.reset(base32)
        abf.reset(basebf)
        o_x1 = a32.alloc(8 * TBC)
        o_ts = a32.alloc(TBC)
        o_ts2 = a32.alloc(TBC)
        o_zs = a32.alloc(TBC)
        o_x2s = a32.alloc(2 * TBC)
        o_sqC = abf.alloc(1024)
        o_h0 = abf.alloc(8 * TBC)
        o_hf = abf.alloc(8 * TBC)
        o_a = abf.alloc(32 * TBC)
        o_ms = o_a
        o_res = o_a
        o_i1 = abf.alloc(TBC)
        o_m1 = abf.alloc(TBC)
        o_w1b = abf.alloc(4 * 1024)
        o_w2b = abf.alloc(2 * 1024)

        def cdump(src_off, bf=True):
            for kt in range(2):
                vx2 = vv(lambda e, kt=kt, s=src_off, bf=bf: e.tensor_copy(
                    a32.v(o_x2s + (kt % 2) * TBC, TBC),
                    abf.v(s + kt * TBC, TBC) if bf else a32.v(s + kt * TBC, TBC)),
                    True)
                w("sync", "V", vx2)
                dma("sync", "dX2", _x2o_cur[128 * kt:128 * (kt + 1),
                                            _f0_cur:_f0_cur + TBC],
                    a32.v(o_x2s + (kt % 2) * TBC, TBC))

        for rs in range(2):
            rs_out = rs1out if rs == 0 else rs2out
            x_rs = x_rs1 if rs == 0 else x_rs2
            x2o = x2a if rs == 0 else x2b
            w("sync", "CC", ccr[rs])
            for hb in range(2):
                f0 = hb * TBC
                _f0_cur = f0
                _x2o_cur = x2o
                w("sync", "V", p.cnt["V"])
                w("sync", "S", p.cnt["S"])
                cms, cx1 = [], []
                for kt in range(8):
                    cms.append(dma("sync", "dMS", abf.v(o_ms + kt * TBC, TBC),
                                   rs_out[128 * kt:128 * (kt + 1), f0:f0 + TBC]))
                for kt in range(8):
                    cx1.append(dma("sync", "dMS", a32.v(o_x1 + kt * TBC, TBC),
                                   x_rs[128 * kt:128 * (kt + 1), f0:f0 + TBC]))
                w("scalar", "dMS", cx1[-1])
                w("vector", "dMS", cx1[-1])
                w("tensor", "dMS", cx1[-1])
                if cstop == 0:
                    cdump(o_ms)
                    continue
                tbc = ln_stats(lambda kt: abf.v(o_ms + kt * TBC, TBC), TBC, o_sqC)
                vap = ln_apply(lambda kt: abf.v(o_ms + kt * TBC, TBC),
                               lambda kt: abf.v(o_h0 + kt * TBC, TBC), TBC, tbc)
                if cstop == 1:
                    cdump(o_h0)
                    continue
                vx1 = None
                for kt in range(8):
                    if cstop == 22:
                        vx1 = vv(lambda e, kt=kt: e.tensor_tensor(
                            out=a32.v(o_ts, TBC),
                            in0=a32.v(o_x1 + kt * TBC, TBC),
                            in1=abf.v(o_h0 + kt * TBC, TBC), op=ALU.add), True)
                    else:
                        vx1 = vv(lambda e, kt=kt: e.tensor_tensor(
                            out=a32.v(o_x1 + kt * TBC, TBC),
                            in0=a32.v(o_x1 + kt * TBC, TBC),
                            in1=abf.v(o_h0 + kt * TBC, TBC), op=ALU.add), True)
                if cstop in (21, 22):
                    w("sync", "V", vx1)
                    for kt in range(2):
                        vx2 = vv(lambda e, kt=kt: e.tensor_copy(
                            a32.v(o_x2s + (kt % 2) * TBC, TBC), a32.v(o_ts, TBC)),
                            True)
                        w("sync", "V", vx2)
                        dma("sync", "dX2", x2o[128 * kt:128 * (kt + 1), f0:f0 + TBC],
                            a32.v(o_x2s + (kt % 2) * TBC, TBC))
                    continue
                w("scalar", "V", vx1)
                sx1 = None
                for kt in range(8):
                    sx1 = ss(lambda e, kt=kt: e.copy(
                        out=abf.v(o_ms + kt * TBC, TBC),
                        in_=a32.v(o_x1 + kt * TBC, TBC)), True)
                w("vector", "S", sx1)
                w("tensor", "S", sx1)
                if cstop == 23:
                    for kt in range(2):
                        sx2 = ss(lambda e, kt=kt: e.copy(
                            out=a32.v(o_x2s + (kt % 2) * TBC, TBC),
                            in_=abf.v(o_ms + kt * TBC, TBC)), True)
                        w("sync", "S", sx2)
                        dma("sync", "dX2", x2o[128 * kt:128 * (kt + 1), f0:f0 + TBC],
                            a32.v(o_x2s + (kt % 2) * TBC, TBC))
                    continue
                if cstop == 2:
                    cdump(o_ms)
                    continue
                tbc = ln_stats(lambda kt: abf.v(o_ms + kt * TBC, TBC), TBC, o_sqC)
                vap = ln_apply(lambda kt: abf.v(o_ms + kt * TBC, TBC),
                               lambda kt: abf.v(o_h0 + kt * TBC, TBC), TBC, tbc)
                w("scalar", "T", tbc)
                w("scalar", "V", vap)
                ss(lambda e: e.copy(out=abf.v(o_i1, TBC), in_=pE[:, :]))
                si1 = ss(lambda e: e.copy(out=abf.v(o_m1, TBC), in_=pF[:, :]), True)
                w("vector", "S", si1)
                if cstop == 3:
                    cdump(o_h0)
                    continue
                tbc = ln_stats(lambda kt: abf.v(o_h0 + kt * TBC, TBC), TBC, o_sqC)
                vap = ln_apply(lambda kt: abf.v(o_h0 + kt * TBC, TBC),
                               lambda kt: abf.v(o_hf + kt * TBC, TBC), TBC, tbc)
                vhf = None
                for kt in range(8):
                    vhf = vv(lambda e, kt=kt: e.tensor_scalar(
                        out=abf.v(o_hf + kt * TBC, TBC),
                        in0=abf.v(o_hf + kt * TBC, TBC),
                        scalar1=small[:, 24 + kt:24 + kt + 1],
                        scalar2=small[:, 32 + kt:32 + kt + 1],
                        op0=ALU.mult, op1=ALU.add), True)
                if cstop == 4:
                    cdump(o_hf)
                    continue
                tq = {}
                sfev = None
                cw1 = {}
                for i in range(4):
                    q0, k0 = divmod(i, 8)
                    cw1[i] = dma("gpsimd", f"dF{i % 4}",
                                 abf.v(o_w1b + (i % 4) * 1024, 1024),
                                 w1_d[128 * k0:128 * (k0 + 1),
                                      1024 * q0:1024 * (q0 + 1)])
                for q in range(4):
                    if sfev is not None:
                        w("tensor", "S", sfev)
                    tlast = None
                    for kt in range(8):
                        widx = q * 8 + kt
                        w("tensor", f"dF{widx % 4}", cw1[widx])
                        if q == 0 and kt == 0:
                            w("tensor", "V", vhf)
                        for fj in range(8):
                            tlast = tt(lambda e, kt=kt, fj=fj, widx=widx: e.matmul(
                                pbank[fj][:, :],
                                abf_t[:, o_w1b + (widx % 4) * 1024 + 128 * fj:
                                      o_w1b + (widx % 4) * 1024 + 128 * (fj + 1)],
                                abf.v(o_hf + kt * TBC, TBC),
                                start=(kt == 0), stop=(kt == 7)), True)
                        tq[widx] = tlast
                        if widx + 4 < 32:
                            nq, nk = divmod(widx + 4, 8)
                            w("gpsimd", "T", tq[widx])
                            cw1[widx + 4] = dma(
                                "gpsimd", f"dF{widx % 4}",
                                abf.v(o_w1b + (widx % 4) * 1024, 1024),
                                w1_d[128 * nk:128 * (nk + 1),
                                     1024 * nq:1024 * (nq + 1)])
                    w("scalar", "T", tlast)
                    for fj in range(8):
                        ft = q * 8 + fj
                        sfev = ss(lambda e, fj=fj, ft=ft: e.activation(
                            out=abf.v(o_a + ft * TBC, TBC),
                            in_=pbank[fj][:, :], func=AF.Silu,
                            bias=small[:, 40 + ft:40 + ft + 1], scale=1.0), True)
                if cstop == 5:
                    cdump(o_a)
                    continue
                w("tensor", "S", sfev)
                t2q = {}
                t2last = None
                cw2 = {0: dma("gpsimd", "dW20", abf.v(o_w2b, 1024), w2_d[0:128, :]),
                       1: dma("gpsimd", "dW21", abf.v(o_w2b + 1024, 1024),
                              w2_d[128:256, :])}
                for ktf in range(32):
                    w("tensor", f"dW2{ktf % 2}", cw2[ktf])
                    for ot in range(8):
                        t2last = tt(lambda e, ktf=ktf, ot=ot: e.matmul(
                            pbank[ot][:, :],
                            abf_t[:, o_w2b + (ktf % 2) * 1024 + 128 * ot:
                                  o_w2b + (ktf % 2) * 1024 + 128 * (ot + 1)],
                            abf.v(o_a + ktf * TBC, TBC),
                            start=(ktf == 0), stop=(ktf == 31)), True)
                    t2q[ktf] = t2last
                    if ktf + 2 < 32:
                        w("gpsimd", "T", t2q[ktf])
                        cw2[ktf + 2] = dma(
                            "gpsimd", f"dW2{ktf % 2}",
                            abf.v(o_w2b + (ktf % 2) * 1024, 1024),
                            w2_d[128 * (ktf + 2):128 * (ktf + 3), :])
                if cstop == 6:
                    w("scalar", "T", t2last)
                    for kt in range(2):
                        sx2 = ss(lambda e, kt=kt: e.copy(
                            out=a32.v(o_x2s + (kt % 2) * TBC, TBC),
                            in_=pbank[kt][:, :]), True)
                        w("sync", "S", sx2)
                        dma("sync", "dX2", x2o[128 * kt:128 * (kt + 1), f0:f0 + TBC],
                            a32.v(o_x2s + (kt % 2) * TBC, TBC))
                    continue
                w("scalar", "T", t2last)
                vres = None
                for ot in range(8):
                    if vres is not None:
                        w("scalar", "V", vres)
                    szs = ss(lambda e, ot=ot: e.activation(
                        out=a32.v(o_zs, TBC), in_=pbank[ot][:, :], func=AF.Identity,
                        bias=small[:, 72 + ot:72 + ot + 1], scale=0.5), True)
                    vv(lambda e, ot=ot: e.tensor_tensor(
                        out=a32.v(o_ts, TBC), in0=a32.v(o_x1 + ot * TBC, TBC),
                        in1=abf.v(o_i1, TBC), op=ALU.mult))
                    vv(lambda e: e.tensor_tensor(
                        out=a32.v(o_ts2, TBC), in0=a32.v(o_ts, TBC),
                        in1=abf.v(o_m1, TBC), op=ALU.subtract))
                    w("vector", "S", szs)
                    vres = vv(lambda e, ot=ot: e.tensor_tensor(
                        out=abf.v(o_res + ot * TBC, TBC), in0=a32.v(o_ts2, TBC),
                        in1=a32.v(o_zs, TBC), op=ALU.add), True)
                tbc = ln_stats(lambda kt: abf.v(o_res + kt * TBC, TBC), TBC, o_sqC)
                vap = ln_apply(lambda kt: abf.v(o_res + kt * TBC, TBC),
                               lambda kt: abf.v(o_hf + kt * TBC, TBC), TBC, tbc)
                cst = None
                for kt in range(8):
                    if kt >= 2:
                        w("vector", "dX2", cst)
                    vx2 = vv(lambda e, kt=kt: e.tensor_tensor(
                        out=a32.v(o_x2s + (kt % 2) * TBC, TBC),
                        in0=a32.v(o_x1 + kt * TBC, TBC),
                        in1=abf.v(o_hf + kt * TBC, TBC), op=ALU.add), True)
                    w("sync", "V", vx2)
                    cst = dma("sync", "dX2",
                              x2o[128 * kt:128 * (kt + 1), f0:f0 + TBC],
                              a32.v(o_x2s + (kt % 2) * TBC, TBC))

        for s in DMA_SEMS:
            w("sync", s, p.cnt[s])
        w("sync", "CC", p.cnt["CC"])
        return nc, p


def _finish(nc, p):
    with nc.Block() as block:
        @block.sync
        def _(e):
            p.replay("sync", e)

        @block.vector
        def _(e):
            p.replay("vector", e)

        @block.scalar
        def _(e):
            p.replay("scalar", e)

        @block.tensor
        def _(e):
            p.replay("tensor", e)

        @block.gpsimd
        def _(e):
            p.replay("gpsimd", e)
    return nc


_TOK = {
    (0, 0): 1024, (0, 1): 0, (1, 0): 2048, (1, 1): 3072,
}


def make_in_maps(inputs):
    x = np.asarray(inputs["L0_attn_x"])
    Wi = np.asarray(inputs["L0_attn_Wi"])
    cw = np.asarray(inputs["L0_attn_conv_w"])
    cb = np.asarray(inputs["L0_attn_conv_b"])
    Wx = np.asarray(inputs["L0_attn_Wx"])
    Wdt = np.asarray(inputs["L0_attn_Wdt"])
    bdt = np.asarray(inputs["L0_attn_bdt"])
    Alog = np.asarray(inputs["L0_attn_A_log"])
    Dp = np.asarray(inputs["L0_attn_Dp"])
    Wo = np.asarray(inputs["L0_attn_Wo"])
    g = np.asarray(inputs["L0_ffn_ln_gamma"])
    be = np.asarray(inputs["L0_ffn_ln_beta"])
    q1 = np.asarray(inputs["L0_ffn_fc1_q"])
    s1_ = np.asarray(inputs["L0_ffn_fc1_scales"])
    b1 = np.asarray(inputs["L0_ffn_fc1_bias"])
    q2 = np.asarray(inputs["L0_ffn_fc2_q"])
    s2_ = np.asarray(inputs["L0_ffn_fc2_scales"])
    b2 = np.asarray(inputs["L0_ffn_fc2_bias"])

    def u8(q):
        v = q.view(np.uint8).reshape(q.shape[0], q.shape[1], 4)[:, :, 0]
        return np.ascontiguousarray(v)

    def tile8(vec):
        return np.ascontiguousarray(vec.reshape(8, 128).T)

    q1u, q2u = u8(q1), u8(q2)
    in_maps = []
    for c in range(8):
        b, dh = c // 2, c % 2
        sl = slice(dh * DH, (dh + 1) * DH)
        xf = np.ascontiguousarray(x[b].T.astype(np.float32))
        al = Alog[sl].reshape(8, 128, NST).transpose(1, 0, 2).reshape(128, 128)
        d = dict(
            x_f=xf,
            x_rs1=np.ascontiguousarray(xf[:, _TOK[(dh, 0)]:_TOK[(dh, 0)] + TBB]),
            x_rs2=np.ascontiguousarray(xf[:, _TOK[(dh, 1)]:_TOK[(dh, 1)] + TBB]),
            wi_t=np.ascontiguousarray(
                np.concatenate([Wi[sl], Wi[2 * DH + dh * DH:2 * DH + (dh + 1) * DH]],
                               axis=0).T),
            wx_t=np.ascontiguousarray(Wx[:, sl].T),
            wdt_t=np.ascontiguousarray(Wdt[sl].T),
            wo_t=np.ascontiguousarray(Wo[:, sl].T),
            convw=np.ascontiguousarray(cw[sl, 0, :]),
            convb=tile8(cb[sl]),
            bdt_t=tile8(bdt[sl]),
            alog=np.ascontiguousarray(al),
            dp_t=tile8(Dp[sl]),
            gam=tile8(g),
            bet=tile8(be),
            fc1b=np.ascontiguousarray(b1.reshape(32, 128).T),
            fc2b=tile8(b2),
            fc1q=q1u,
            fc2q=q2u,
            fc1s=np.ascontiguousarray(s1_),
            fc2s=np.ascontiguousarray(s2_),
            ones1=np.ones((1, 128), np.float32),
            onesc=np.ones((128, 1), np.float32),
        )
        in_maps.append(d)
    return in_maps


def run_raw(inputs, dev=None, upto=None, cstop=99):
    dev = DEV if dev is None else dev
    upto = BUILD_UPTO if upto is None else upto
    nc, p = build_nc(dev=dev, upto=upto, cstop=cstop)
    _finish(nc, p)
    in_maps = make_in_maps(inputs)
    return run_bass_kernel_spmd(nc, in_maps, core_ids=list(range(8)))


def kernel(**inputs):
    res = run_raw(inputs, dev=0, upto="C")
    out = np.empty((4, L, DM), np.float32)
    for c in range(8):
        b, dh = c // 2, c % 2
        r = res.results[c]
        t1 = _TOK[(dh, 0)]
        t2 = _TOK[(dh, 1)]
        out[b, t1:t1 + TBB, :] = np.asarray(r["x2a"]).T
        out[b, t2:t2 + TBB, :] = np.asarray(r["x2b"]).T
    return out


# version 11 (final)
# speedup vs baseline: 1.0560x; 1.0396x over previous
import numpy as np
import concourse.bass as bass
import concourse.mybir as mybir
from concourse.bass_utils import run_bass_kernel_spmd
from contextlib import ExitStack

F32 = mybir.dt.float32
BF16 = mybir.dt.bfloat16
U8 = mybir.dt.uint8
AF = mybir.ActivationFunctionType
ALU = mybir.AluOpType

L = 4096
DM = 1024
DH = 1024
NST = 16
DTR = 64
TBA = 512
TBB = 1024
TBC = 512
EPS = 1e-5
PAIRS = [[0, 1], [2, 3], [4, 5], [6, 7]]

DEV = 0
BUILD_UPTO = "C"

ENGINES = ("sync", "vector", "scalar", "tensor", "gpsimd")


class Plan:
    def __init__(self):
        self.q = {e: [] for e in ENGINES}
        self.cnt = {}
        self.sems = {}

    def add_sem(self, name, handle):
        self.sems[name] = handle
        self.cnt[name] = 0

    def em(self, eng, fn, inc=None, n=1):
        if inc is None:
            self.q[eng].append((fn, None, 0))
            return None
        self.cnt[inc] += n
        self.q[eng].append((fn, inc, n))
        return self.cnt[inc]

    def wait(self, eng, sem, val):
        if val is None or val <= 0:
            return
        self.q[eng].append((None, sem, val))

    def replay(self, eng, e):
        for fn, sem, val in self.q[eng]:
            if fn is None:
                e.wait_ge(self.sems[sem], val)
            else:
                ins = fn(e)
                if sem is not None:
                    ins.then_inc(self.sems[sem], val)


class Arena:
    def __init__(self, t, cols):
        self.t, self.cols, self.off = t, cols, 0

    def alloc(self, n):
        n = (n + 15) // 16 * 16
        assert self.off + n <= self.cols, f"arena overflow {self.off}+{n}>{self.cols}"
        o = self.off
        self.off += n
        return o

    def v(self, o, n):
        return self.t[:, o:o + n]

    def r(self, o, n):
        return self.t[0:1, o:o + n]

    def mark(self):
        return self.off

    def reset(self, m):
        self.off = m


def build_nc(dev=0, upto="C", cstop=99):
    nc = bass.Bass("TRN2", target_bir_lowering=False,
                   detect_race_conditions=False)

    def din(name, shape, dt=F32):
        return nc.dram_tensor(name, shape, dt, kind="ExternalInput").ap()

    def dout(name, shape, dt=F32):
        return nc.dram_tensor(name, shape, dt, kind="ExternalOutput").ap()

    def dint(name, shape, dt):
        return nc.dram_tensor(name, shape, dt).ap()

    x_f = din("x_f", [DM, L])
    x_rs1 = din("x_rs1", [DM, TBB])
    x_rs2 = din("x_rs2", [DM, TBB])
    wi_t = din("wi_t", [DM, 2 * DH])
    wx_t = din("wx_t", [DH, 96])
    wdt_t = din("wdt_t", [DTR, DH])
    wo_t = din("wo_t", [DH, DM])
    convw = din("convw", [DH, 4])
    convb = din("convb", [128, 8])
    bdt_t = din("bdt_t", [128, 8])
    alog = din("alog", [128, 128])
    dp_t = din("dp_t", [128, 8])
    gam = din("gam", [128, 8])
    bet = din("bet", [128, 8])
    fc1b = din("fc1b", [128, 32])
    fc2b = din("fc2b", [128, 8])
    fc1q = din("fc1q", [DM, 4096], U8)
    fc2q = din("fc2q", [4096, DM], U8)
    fc1s = din("fc1s", [8, 4096])
    fc2s = din("fc2s", [32, DM])
    ones1 = din("ones1", [1, 128])
    onesc = din("onesc", [128, 1])

    x2a = dout("x2a", [DM, TBB])
    x2b = dout("x2b", [DM, TBB])
    if dev >= 1:
        dv_xp = dout("dv_xp", [DH, L], BF16)
        dv_z = dout("dv_z", [DH, L], BF16)
        dv_xdbl = dout("dv_xdbl", [96, L])
    if dev >= 2:
        dv_y = dout("dv_y", [DH, L])
        dv_rs1 = dout("dv_rs1", [DM, TBB], BF16)
        dv_rs2 = dout("dv_rs2", [DM, TBB], BF16)
    if dev >= 3:
        dv_dl = dout("dv_dl", [128, 8 * TBB], BF16)
        dv_du = dout("dv_du", [128, 8 * TBB], BF16)
        dv_y0 = dout("dv_y0", [128, 8 * TBB], F32)
        dv_cr = dout("dv_cr", [128, 128], F32)
        dv_bg = dout("dv_bg", [128, 4 * TBB], BF16)
        dv_bg2 = dout("dv_bg2", [96, TBB], BF16)
        dv_fg = dout("dv_fg", [128, TBB], F32)
        dv_wdt = dout("dv_wdt", [DTR, DH], BF16)

    w1_d = dint("w1_d", [DM, 4096], BF16)
    w2_d = dint("w2_d", [4096, DM], BF16)
    z_d = dint("z_d", [DH, L], BF16)
    xp_d = dint("xp_d", [DH, L], BF16)
    xdbl_p = dint("xdbl_p", [96, L], F32)
    xdbl_s = dint("xdbl_s", [96, L], F32)
    rs1in = dint("rs1in", [2 * DM, TBB], BF16)
    rs2in = dint("rs2in", [2 * DM, TBB], BF16)
    rs1out = dint("rs1out", [DM, TBB], BF16)
    rs2out = dint("rs2out", [DM, TBB], BF16)

    p = Plan()
    es = ExitStack()
    with es:
        AF32C = 15552
        ABFC = 38048
        af32_t = es.enter_context(nc.sbuf_tensor("af32", [128, AF32C], F32))
        abf_t = es.enter_context(nc.sbuf_tensor("abf", [128, ABFC], BF16))
        a32 = Arena(af32_t, AF32C)
        abf = Arena(abf_t, ABFC)

        wo_bf = es.enter_context(nc.sbuf_tensor("wo_bf", [128, 8 * DM], BF16))
        wx_bf = es.enter_context(nc.sbuf_tensor("wx_bf", [128, 8 * 96], BF16))
        wdt_f = es.enter_context(nc.sbuf_tensor("wdt_f", [DTR, DH], F32))
        wdt_bf = es.enter_context(nc.sbuf_tensor("wdt_bf", [DTR, DH], BF16))
        small = es.enter_context(nc.sbuf_tensor("small", [128, 256], F32))
        negA = 80
        cwo = 208
        carry = es.enter_context(nc.sbuf_tensor("carry", [128, 128], F32))
        onebf = es.enter_context(nc.sbuf_tensor("onebf", [1, 128], BF16))
        onecbf = es.enter_context(nc.sbuf_tensor("onecbf", [128, 1], BF16))
        alog_sb = es.enter_context(nc.sbuf_tensor("alog_sb", [128, 128], F32))
        qbuf = es.enter_context(nc.sbuf_tensor("qbuf", [128, 4096], U8))
        xdblb = es.enter_context(nc.sbuf_tensor("xdblb", [96, L], BF16))

        pbank = [es.enter_context(nc.psum_tensor(f"pb{i}", [128, 512], F32))
                 for i in range(8)]
        pA, pB, pC_, pD, pE, pF, pG, pH = pbank

        DMA_SEMS = ("dW", "dR0", "dR1", "dQ", "dQS", "dX0", "dX1", "dZS",
                    "dXD", "dDV", "dXC0", "dXC1", "dXP", "dBC0", "dBC1",
                    "dZL", "dOG", "dMS", "dF0", "dF1", "dF2", "dF3",
                    "dW20", "dW21", "dX2")
        for s in DMA_SEMS + ("V", "S", "T", "CC"):
            p.add_sem(s, es.enter_context(nc.semaphore(s)))

        def dma(q, sem, out, in_):
            return p.em(q, lambda e, o=out, i=in_: e.dma_start(out=o, in_=i),
                        inc=sem, n=16)

        def vv(fn, inc=False):
            return p.em("vector", fn, "V" if inc else None)

        def ss(fn, inc=False):
            return p.em("scalar", fn, "S" if inc else None)

        def tt(fn, inc=False):
            return p.em("tensor", fn, "T" if inc else None)

        def w(eng, sem, val):
            p.wait(eng, sem, val)

        cstt = es.enter_context(nc.sbuf_tensor("cstt", [128, 4], F32))
        CONSTS = [-128.0, 1.0, EPS, 0.0]
        for i, val in enumerate(CONSTS):
            nc.const_aps.register(F32, val, cstt[:, i:i + 1])
        vcst = None
        for i, val in enumerate(CONSTS):
            vcst = p.em("vector", lambda e, i=i, val=val, t=cstt: e.memset(
                t[:, i:i + 1], val), "V", 1)
        p.wait("scalar", "V", vcst)
        p.wait("tensor", "V", vcst)

        def phase_barrier():
            for eng in ENGINES:
                for s in ("V", "S", "T"):
                    w(eng, s, p.cnt[s])
                for s in DMA_SEMS:
                    w(eng, s, p.cnt[s])

        o_stw = a32.alloc(4096)
        o_sbf = abf.alloc(2048)
        base32, basebf = a32.mark(), abf.mark()
        o_wi = abf.alloc(8 * 2 * DH)
        basebf_wi = abf.mark()

        def wi_bf_sl(a, b):
            return abf_t[:, o_wi + a:o_wi + b]

        dma("sync", "dW", small[:, 0:8], convb)
        dma("sync", "dW", small[:, 8:16], bdt_t)
        dma("sync", "dW", small[:, 16:24], dp_t)
        dma("sync", "dW", small[:, 24:32], gam)
        dma("sync", "dW", small[:, 32:40], bet)
        dma("sync", "dW", small[:, 40:72], fc1b)
        dma("sync", "dW", small[:, 72:80], fc2b)
        dma("sync", "dW", alog_sb[:, :], alog)
        for dt in range(8):
            dma("sync", "dW", small[:, cwo + 4 * dt:cwo + 4 * (dt + 1)],
                convw[128 * dt:128 * (dt + 1), :])
        dma("sync", "dW", wdt_f[:, :], wdt_t)
        dma("sync", "dW", small[:, 250:251], onesc)
        o_r1 = a32.alloc(128)
        c1 = dma("sync", "dW", a32.r(o_r1, 128), ones1)
        w("scalar", "dW", c1)
        ss(lambda e: e.copy(out=onebf[:, :], in_=a32.r(o_r1, 128)))
        ss(lambda e: e.copy(out=onecbf[:, :], in_=small[:, 250:251]))
        ss(lambda e: e.copy(out=wdt_bf[:, :], in_=wdt_f[:, :]))
        ss(lambda e: e.activation(out=small[:, 72:80], in_=small[:, 72:80],
                                  func=AF.Copy, scale=0.5))
        s1 = ss(lambda e: e.activation(out=small[:, negA:negA + 128],
                                       in_=alog_sb[:, :], func=AF.Exp), True)
        w("vector", "S", s1)
        v1 = vv(lambda e: e.tensor_scalar(
            out=small[:, negA:negA + 128], in0=small[:, negA:negA + 128],
            scalar1=-1.0, scalar2=None, op0=ALU.mult), True)
        w("scalar", "V", v1)

        o_ws = a32.alloc(2 * 2048)
        wjobs = ([(wx_t, kt, 96, lambda kt: wx_bf[:, 96 * kt:96 * (kt + 1)])
                  for kt in range(8)] +
                 [(wo_t, kt, DM, lambda kt: wo_bf[:, DM * kt:DM * (kt + 1)])
                  for kt in range(8)] +
                 [(wi_t, kt, 2 * DH,
                   lambda kt: wi_bf_sl(2 * DH * kt, 2 * DH * (kt + 1)))
                  for kt in range(8)])
        scasts = {}
        for gi, (srcd, kt, cols, dstf) in enumerate(wjobs):
            if gi >= 2:
                w("sync", "S", scasts[gi - 2])
            cw1 = dma("sync", f"dR{gi % 2}", a32.v(o_ws + (gi % 2) * 2048, cols),
                      srcd[128 * kt:128 * (kt + 1), :])
            w("scalar", f"dR{gi % 2}", cw1)
            scasts[gi] = ss(lambda e, kt=kt, cols=cols, dstf=dstf,
                            o=o_ws + (gi % 2) * 2048: e.copy(
                out=dstf(kt), in_=a32.v(o, cols)), True)

        o_scr = a32.alloc(4096)
        o_srb = abf.alloc(4096)
        o_qm = abf.alloc(4096)
        o_wr = abf.alloc(4096)

        def dequant(qdram, sdram, wdram, nkt, ncol, s2=None, sdma=None,
                    vlast=None):
            vprev = {}
            for kt in range(nkt):
                if s2 is not None:
                    w("gpsimd", "S", s2)
                cq = dma("gpsimd", "dQ", qbuf[:, 0:ncol],
                         qdram[128 * kt:128 * (kt + 1), :])
                cs = dma("gpsimd", "dQ", a32.r(o_scr, ncol), sdram[kt:kt + 1, :])
                w("scalar", "dQ", cs)
                if vlast is not None:
                    w("scalar", "V", vlast)
                ss(lambda e, n=ncol: e.copy(out=abf.r(o_srb, n), in_=a32.r(o_scr, n)))
                s2 = ss(lambda e, n=ncol: e.activation(
                    out=abf.v(o_qm, n), in_=qbuf[:, 0:n], func=AF.Identity,
                    bias=-128.0, scale=1.0), True)
                if sdma is not None:
                    w("vector", "dQS", sdma)
                for ch in range(ncol // 512):
                    bank = pbank[ch % 2]
                    if (kt, ch) != (0, 0):
                        pv = vprev.get((kt, ch - 2)) or vprev.get(
                            (kt - 1, ncol // 512 - 2 + ch))
                        w("tensor", "V", pv)
                    if ch == 0:
                        w("tensor", "S", s2)
                    t1 = tt(lambda e, b=bank, ch=ch: e.matmul(
                        b[:, :], onebf[:, :], abf.r(o_srb + 512 * ch, 512),
                        start=True, stop=True), True)
                    w("vector", "T", t1)
                    w("vector", "S", s2)
                    vlast = vv(lambda e, b=bank, ch=ch: e.tensor_tensor(
                        out=abf.v(o_wr + 512 * ch, 512),
                        in0=abf.v(o_qm + 512 * ch, 512), in1=b[:, :],
                        op=ALU.mult), True)
                    vprev[(kt, ch)] = vlast
                w("sync", "V", vlast)
                sdma = dma("sync", "dQS", wdram[128 * kt:128 * (kt + 1), :],
                           abf.v(o_wr, ncol))
            return s2, sdma, vlast

        dq_state = dequant(fc1q, fc1s, w1_d, 8, 4096)
        dequant(fc2q, fc2s, w2_d, 32, DM, *dq_state)
        phase_barrier()
        a32.reset(base32)
        abf.reset(basebf)

        def ln_stats(get_x, tb, o_sq, D=1024):
            nch = tb // 512
            w("tensor", "V", p.cnt["V"])
            w("tensor", "S", p.cnt["S"])
            t1 = None
            for ch in range(nch):
                bs, bq = (pE, pF) if ch == 0 else (pG, pH)
                tq_prev = None
                for kt in range(8):
                    if tq_prev is not None:
                        w("scalar", "T", tq_prev)
                    s1 = ss(lambda e, kt=kt, ch=ch: e.activation(
                        out=abf.v(o_sq + (kt % 2) * 512, 512),
                        in_=get_x(kt)[:, 512 * ch:512 * (ch + 1)],
                        func=AF.Square), True)
                    w("tensor", "S", s1)
                    tt(lambda e, kt=kt, ch=ch, bs=bs: e.matmul(
                        bs[0:1, :], onecbf[:, :],
                        get_x(kt)[:, 512 * ch:512 * (ch + 1)],
                        start=(kt == 0), stop=(kt == 7)))
                    t1 = tt(lambda e, kt=kt, bq=bq: e.matmul(
                        bq[0:1, :], onecbf[:, :], abf.v(o_sq + (kt % 2) * 512, 512),
                        start=(kt == 0), stop=(kt == 7)), True)
                    if kt >= 1:
                        tq_prev = t1
            w("vector", "T", t1)
            vlast = None
            for ch in range(nch):
                bs, bq = (pE, pF) if ch == 0 else (pG, pH)
                o = 512 * ch
                vv(lambda e, bs=bs, o=o: e.tensor_scalar(
                    out=a32.r(o_stw + o, 512), in0=bs[0:1, :], scalar1=1.0 / D,
                    scalar2=None, op0=ALU.mult))
                vv(lambda e, bq=bq, o=o: e.tensor_scalar(
                    out=a32.r(o_stw + 1024 + o, 512), in0=bq[0:1, :],
                    scalar1=1.0 / D, scalar2=None, op0=ALU.mult))
                vv(lambda e, o=o: e.tensor_tensor(
                    out=a32.r(o_stw + 2048 + o, 512), in0=a32.r(o_stw + o, 512),
                    in1=a32.r(o_stw + o, 512), op=ALU.mult))
                vlast = vv(lambda e, o=o: e.tensor_tensor(
                    out=a32.r(o_stw + 1024 + o, 512),
                    in0=a32.r(o_stw + 1024 + o, 512),
                    in1=a32.r(o_stw + 2048 + o, 512), op=ALU.subtract), True)
            w("scalar", "V", vlast)
            slast = None
            for ch in range(nch):
                o = 512 * ch
                ss(lambda e, o=o: e.activation(
                    out=a32.r(o_stw + 2048 + o, 512),
                    in_=a32.r(o_stw + 1024 + o, 512),
                    func=AF.Ln, bias=EPS, scale=1.0))
                slast = ss(lambda e, o=o: e.activation(
                    out=a32.r(o_stw + 2048 + o, 512),
                    in_=a32.r(o_stw + 2048 + o, 512),
                    func=AF.Exp, bias=0.0, scale=-0.5), True)
            w("vector", "S", slast)
            vlast = None
            for ch in range(nch):
                o = 512 * ch
                vlast = vv(lambda e, o=o: e.tensor_tensor(
                    out=a32.r(o_stw + 3072 + o, 512), in0=a32.r(o_stw + o, 512),
                    in1=a32.r(o_stw + 2048 + o, 512), op=ALU.mult), True)
            w("scalar", "V", vlast)
            slast = None
            for ch in range(nch):
                o = 512 * ch
                ss(lambda e, o=o: e.copy(out=abf.r(o_sbf + o, 512),
                                         in_=a32.r(o_stw + 2048 + o, 512)))
                slast = ss(lambda e, o=o: e.copy(out=abf.r(o_sbf + 1024 + o, 512),
                                                 in_=a32.r(o_stw + 3072 + o, 512)),
                           True)
            w("tensor", "S", slast)
            for ch in range(nch):
                bs, bq = (pE, pF) if ch == 0 else (pG, pH)
                o = 512 * ch
                tt(lambda e, bs=bs, o=o: e.matmul(
                    bs[:, :], onebf[:, :], abf.r(o_sbf + o, 512),
                    start=True, stop=True))
                t1 = tt(lambda e, bq=bq, o=o: e.matmul(
                    bq[:, :], onebf[:, :], abf.r(o_sbf + 1024 + o, 512),
                    start=True, stop=True), True)
            return t1

        def ln_apply(get_x, put, tb, t_bc):
            w("vector", "T", t_bc)
            vlast = None
            for kt in range(8):
                for ch in range(tb // 512):
                    bs, bq = (pE, pF) if ch == 0 else (pG, pH)
                    sl = slice(512 * ch, 512 * (ch + 1))
                    vv(lambda e, kt=kt, sl=sl, bs=bs: e.tensor_tensor(
                        out=put(kt)[:, sl], in0=get_x(kt)[:, sl], in1=bs[:, :],
                        op=ALU.mult))
                    vlast = vv(lambda e, kt=kt, sl=sl, bq=bq: e.tensor_tensor(
                        out=put(kt)[:, sl], in0=put(kt)[:, sl], in1=bq[:, :],
                        op=ALU.subtract), True)
            return vlast

        abf.reset(basebf_wi)
        o_xkt = a32.alloc(2 * TBA)
        o_xpre = a32.alloc(8 * (TBA + 3) + 16)
        o_cacc = a32.alloc(2 * TBA)
        o_xds = a32.alloc(TBA)
        o_xbf = abf.alloc(2 * 8 * TBA)
        o_sq = abf.alloc(1024)
        o_zb = abf.alloc(8 * TBA)
        o_xpb = abf.alloc(8 * TBA)

        def xbuf(tba, kt):
            return abf.v(o_xbf + (tba % 2) * 8 * TBA + kt * TBA, TBA)

        vhalo = None
        for dt in range(8):
            vhalo = vv(lambda e, dt=dt: e.memset(
                a32.v(o_xpre + dt * (TBA + 3), 3), 0.0), True)

        last_xz = {}
        vap_ms = {}
        cast_ms = {}
        tbc_ms = {}
        evs_g = {}
        sxd = None
        txd_ms = {}
        zstore_c = None
        xds_dma = None

        xkt_cast = {}
        xkt_gidx = [0]

        def emit_loads(tba):
            t0 = tba * TBA
            for kt in range(8):
                gi = xkt_gidx[0]
                if kt == 0 and tba >= 2:
                    w("scalar", "T", last_xz[tba - 2])
                    w("scalar", "V", vap_ms[tba - 2])
                if gi >= 2:
                    w("sync", "S", xkt_cast[gi - 2])
                cx = dma("sync", f"dX{gi % 2}",
                         a32.v(o_xkt + (gi % 2) * TBA, TBA),
                         x_f[128 * kt:128 * (kt + 1), t0:t0 + TBA])
                w("scalar", f"dX{gi % 2}", cx)
                xkt_cast[gi] = ss(lambda e, tba=tba, kt=kt,
                                  o=o_xkt + (gi % 2) * TBA: e.copy(
                    out=xbuf(tba, kt), in_=a32.v(o, TBA)), True)
                xkt_gidx[0] += 1
            return xkt_cast[xkt_gidx[0] - 1]

        cast_ms[0] = emit_loads(0)
        tbc_ms[0] = ln_stats(lambda kt: xbuf(0, kt), TBA, o_sq)

        for tba in range(8):
            t0 = tba * TBA
            vap = ln_apply(lambda kt, tba=tba: xbuf(tba, kt),
                           lambda kt, tba=tba: xbuf(tba, kt), TBA, tbc_ms[tba])
            vap_ms[tba] = vap
            if tba + 1 < 8:
                cast_ms[tba + 1] = emit_loads(tba + 1)
            w("scalar", "V", vhalo)
            if zstore_c is not None:
                w("scalar", "dZS", zstore_c)
            for ft in range(16):
                gft = tba * 16 + ft
                bank = pbank[ft % 4]
                if gft >= 4:
                    w("tensor", "S", evs_g[gft - 4])
                tmm = None
                for kt in range(8):
                    if ft == 0 and kt == 0:
                        w("tensor", "V", vap)
                    tmm = tt(lambda e, b=bank, kt=kt, ft=ft, tba=tba: e.matmul(
                        b[:, 0:TBA],
                        wi_bf_sl(2 * DH * kt + 128 * ft,
                                 2 * DH * kt + 128 * (ft + 1)),
                        xbuf(tba, kt),
                        start=(kt == 0), stop=(kt == 7)), True)
                w("scalar", "T", tmm)
                if ft < 8:
                    evs_g[gft] = ss(lambda e, b=bank, ft=ft: e.copy(
                        out=a32.v(o_xpre + ft * (TBA + 3) + 3, TBA),
                        in_=b[:, 0:TBA]), True)
                else:
                    evs_g[gft] = ss(lambda e, b=bank, ft=ft: e.activation(
                        out=abf.v(o_zb + (ft - 8) * TBA, TBA), in_=b[:, 0:TBA],
                        func=AF.Silu), True)
            last_xz[tba] = tmm
            if tba + 1 < 8:
                tbc_ms[tba + 1] = ln_stats(
                    lambda kt, tba=tba: xbuf(tba + 1, kt), TBA, o_sq)
            ssl = None
            sslm = {}
            for dt in range(8):
                ob = o_xpre + dt * (TBA + 3)
                oc = o_cacc + (dt % 2) * TBA
                w("vector", "S", evs_g[tba * 16 + dt])
                if dt >= 2:
                    w("vector", "S", sslm[dt - 2])
                vv(lambda e, dt=dt, ob=ob, oc=oc: e.tensor_scalar(
                    out=a32.v(oc, TBA), in0=a32.v(ob + 0, TBA),
                    scalar1=small[:, cwo + 4 * dt:cwo + 4 * dt + 1],
                    scalar2=None, op0=ALU.mult))
                for j in (1, 2, 3):
                    vv(lambda e, dt=dt, ob=ob, j=j, oc=oc: e.scalar_tensor_tensor(
                        out=a32.v(oc, TBA), in0=a32.v(ob + j, TBA),
                        scalar=small[:, cwo + 4 * dt + j:cwo + 4 * dt + j + 1],
                        in1=a32.v(oc, TBA), op0=ALU.mult, op1=ALU.add))
                vhalo = vv(lambda e, ob=ob: e.tensor_copy(
                    a32.v(ob, 3), a32.v(ob + TBA, 3)), True)
                w("scalar", "V", vhalo)
                if dt == 0 and tba >= 1:
                    w("scalar", "T", txd_ms[tba - 1])
                ssl = ss(lambda e, dt=dt, oc=oc: e.activation(
                    out=abf.v(o_xpb + dt * TBA, TBA), in_=a32.v(oc, TBA),
                    func=AF.Silu, bias=small[:, dt:dt + 1], scale=1.0), True)
                sslm[dt] = ssl
            w("sync", "S", ssl)
            for dt in range(8):
                dma("sync", "dZS", z_d[128 * dt:128 * (dt + 1), t0:t0 + TBA],
                    abf.v(o_zb + dt * TBA, TBA))
                zstore_c = dma("sync", "dZS",
                               xp_d[128 * dt:128 * (dt + 1), t0:t0 + TBA],
                               abf.v(o_xpb + dt * TBA, TBA))
            if sxd is not None:
                w("tensor", "S", sxd)
            txd = None
            for kt in range(8):
                if kt == 0:
                    w("tensor", "S", ssl)
                txd = tt(lambda e, kt=kt: e.matmul(
                    pG[0:96, 0:TBA], wx_bf[:, 96 * kt:96 * (kt + 1)],
                    abf.v(o_xpb + kt * TBA, TBA),
                    start=(kt == 0), stop=(kt == 7)), True)
            txd_ms[tba] = txd
            w("scalar", "T", txd)
            if xds_dma is not None:
                w("scalar", "dXD", xds_dma)
            sxd = ss(lambda e: e.copy(out=af32_t[0:96, o_xds:o_xds + TBA],
                                      in_=pG[0:96, 0:TBA]), True)
            w("sync", "S", sxd)
            xds_dma = dma("sync", "dXD", xdbl_p[:, t0:t0 + TBA],
                          af32_t[0:96, o_xds:o_xds + TBA])

        w("gpsimd", "dXD", p.cnt["dXD"])
        cc1 = p.em("gpsimd", lambda e: e.collective_compute(
            "AllReduce", ALU.add, replica_groups=PAIRS,
            ins=[xdbl_p], outs=[xdbl_s]), "CC", 1)

        if upto == "A":
            if dev >= 1:
                w("sync", "CC", cc1)
                dma("sync", "dDV", dv_xdbl[:, :], xdbl_s[:, :])
                dma("sync", "dDV", dv_xp[:, :], xp_d[:, :])
                dma("sync", "dDV", dv_z[:, :], z_d[:, :])
            for s in DMA_SEMS:
                w("sync", s, p.cnt[s])
            w("sync", "CC", p.cnt["CC"])
            return nc, p

        phase_barrier()
        a32.reset(base32)
        abf.reset(basebf)
        o_dA = a32.alloc(2 * TBB)
        o_yb = a32.alloc(8 * TBB)
        o_eb = a32.alloc(512)
        o_xpB = abf.alloc(8 * TBB)
        o_dz = abf.alloc(8 * TBB)
        o_dl = abf.alloc(8 * TBB)
        o_u = abf.alloc(2 * TBB)
        o_h = abf.alloc(2 * TBB)
        o_Bb = abf.alloc(TBB)
        o_Cb = abf.alloc(TBB)
        o_og = abf.alloc(TBB)
        o_bcr = abf.alloc(4 * TBB)

        o_xdc = o_yb
        vz = vv(lambda e: e.memset(carry[:, :], 0.0), True)
        w("gpsimd", "CC", cc1)
        s_xdbl = None
        for chx in range(8):
            if chx >= 2:
                w("gpsimd", "S", s_xdbl)
            cxs = dma("gpsimd", f"dXC{chx % 2}",
                      af32_t[0:96, o_xdc + (chx % 2) * 512:o_xdc + (chx % 2) * 512 + 512],
                      xdbl_s[:, 512 * chx:512 * (chx + 1)])
            w("scalar", f"dXC{chx % 2}", cxs)
            s_xdbl = ss(lambda e, chx=chx: e.copy(
                out=xdblb[:, 512 * chx:512 * (chx + 1)],
                in_=af32_t[0:96, o_xdc + (chx % 2) * 512:o_xdc + (chx % 2) * 512 + 512]),
                True)
        if dev >= 1:
            w("sync", "CC", cc1)
            dma("sync", "dDV", dv_xdbl[:, :], xdbl_s[:, :])
            dma("sync", "dDV", dv_xp[:, :], xp_d[:, :])
            dma("sync", "dDV", dv_z[:, :], z_d[:, :])

        rs_map = {0: (rs2in, 0), 1: (rs1in, 0), 2: (rs1in, DM), 3: (rs2in, DM)}
        ccr = {}
        vg = None
        smo = None
        ccpy_last = None
        for tb in range(4):
            t0 = tb * TBB
            if vg is not None:
                w("sync", "V", vg)
            cxp = []
            for dt in range(8):
                cxp.append(dma("sync", "dXP", abf.v(o_xpB + dt * TBB, TBB),
                               xp_d[128 * dt:128 * (dt + 1), t0:t0 + TBB]))
            dlev = {}
            s_dl = None
            for dt in range(8):
                for ch in range(2):
                    key = dt * 2 + ch
                    bank = pbank[key % 2]
                    if key >= 2:
                        w("tensor", "S", dlev[key - 2])
                    elif tb == 0:
                        w("tensor", "S", s_xdbl)
                    elif smo is not None:
                        w("tensor", "S", smo)
                    tdl = tt(lambda e, b=bank, dt=dt, ch=ch, t0=t0: e.matmul(
                        b[:, :], wdt_bf[:, 128 * dt:128 * (dt + 1)],
                        xdblb[0:DTR, t0 + 512 * ch:t0 + 512 * (ch + 1)],
                        start=True, stop=True), True)
                    w("scalar", "T", tdl)
                    ss(lambda e, b=bank, dt=dt: e.activation(
                        out=a32.v(o_eb, 512), in_=b[:, :], func=AF.Exp,
                        bias=small[:, 8 + dt:8 + dt + 1], scale=1.0))
                    s_dl = ss(lambda e, dt=dt, ch=ch: e.activation(
                        out=abf.v(o_dl + dt * TBB + 512 * ch, 512),
                        in_=a32.v(o_eb, 512), func=AF.Ln, bias=1.0, scale=1.0),
                        True)
                    dlev[key] = s_dl
                    if dev >= 3 and tb == 1 and dt == 0 and ch == 0:
                        s_dl = ss(lambda e: e.copy(out=a32.v(o_dA, 512),
                                                   in_=a32.v(o_eb, 512)), True)
                        s_dl = ss(lambda e, b=bank: e.copy(
                            out=a32.v(o_dA + 512, 512), in_=b[:, :]), True)
                        w("sync", "S", s_dl)
                        dma("sync", "dDV", dv_wdt[:, :], wdt_bf[:, :])
                        cdb = dma("sync", "dDV", dv_fg[:, :],
                                  a32.v(o_dA, TBB))
                        w("scalar", "dDV", cdb)
                        dlev[key] = s_dl
            w("vector", "S", s_dl)
            vdu = None
            w("vector", "dXP", cxp[7])
            for dt in range(8):
                vdu = vv(lambda e, dt=dt: e.tensor_tensor(
                    out=abf.v(o_dz + dt * TBB, TBB),
                    in0=abf.v(o_dl + dt * TBB, TBB),
                    in1=abf.v(o_xpB + dt * TBB, TBB), op=ALU.mult), True)
            vy = None
            yms = {}
            ccpy = {}
            sevBC = None
            if ccpy_last is not None:
                w("vector", "S", ccpy_last)
            if tb == 0:
                w("sync", "S", s_xdbl)
            if dev >= 3 and tb == 1:
                w("sync", "V", vdu)
                dma("sync", "dDV", dv_dl[:, :], abf.v(o_dl, 8 * TBB))
                dma("sync", "dDV", dv_cr[:, :], carry[:, :])
                cdd = dma("sync", "dDV", dv_du[:, :], abf.v(o_dz, 8 * TBB))
                w("vector", "dDV", cdd)
            cz = []
            for n in range(NST):
                rb = (n % 2) * TBB
                rc = (2 + n % 2) * TBB
                if n >= 2:
                    w("sync", "V", yms[(n - 2, 7)])
                cb = dma("sync", f"dBC{n % 2}",
                         abf_t[0:1, o_bcr + rb:o_bcr + rb + TBB],
                         xdblb[DTR + n:DTR + n + 1, t0:t0 + TBB])
                cc_ = dma("sync", f"dBC{n % 2}",
                          abf_t[0:1, o_bcr + rc:o_bcr + rc + TBB],
                          xdblb[DTR + NST + n:DTR + NST + n + 1, t0:t0 + TBB])
                if n > 0:
                    w("scalar", "V", yms[(n - 1, 7)])
                for ch in range(2):
                    w("tensor", f"dBC{n % 2}", cc_)
                    if sevBC is not None:
                        w("tensor", "S", sevBC)
                    tt(lambda e, ch=ch, rb=rb: e.matmul(
                        pE[:, :], onebf[:, :],
                        abf_t[0:1, o_bcr + rb + 512 * ch:o_bcr + rb + 512 * (ch + 1)],
                        start=True, stop=True))
                    tb2 = tt(lambda e, ch=ch, rc=rc: e.matmul(
                        pF[:, :], onebf[:, :],
                        abf_t[0:1, o_bcr + rc + 512 * ch:o_bcr + rc + 512 * (ch + 1)],
                        start=True, stop=True), True)
                    w("scalar", "T", tb2)
                    ss(lambda e, ch=ch: e.copy(out=abf.v(o_Bb + 512 * ch, 512),
                                               in_=pE[:, :]))
                    sevBC = ss(lambda e, ch=ch: e.copy(out=abf.v(o_Cb + 512 * ch, 512),
                                                       in_=pF[:, :]), True)
                if dev >= 3 and tb == 1 and n == 1:
                    vdbg2 = vv(lambda e: e.engine_nop(), True)
                    w("sync", "V", vdbg2)
                    cdy = dma("sync", "dDV", dv_y0[:, :], a32.v(o_yb, 8 * TBB))
                    w("vector", "dDV", cdy)
                for dt in range(8):
                    idx = n * 8 + dt
                    if idx >= 2:
                        w("scalar", "V", yms[((idx - 2) // 8, (idx - 2) % 8)])
                    sdA = ss(lambda e, n=n, dt=dt, idx=idx: e.activation(
                        out=a32.v(o_dA + (idx % 2) * TBB, TBB),
                        in_=abf.v(o_dl + dt * TBB, TBB), func=AF.Exp,
                        scale=small[:, negA + dt * 16 + n:negA + dt * 16 + n + 1]),
                        True)
                    if dt == 0:
                        w("vector", "S", sevBC)
                    if n == 0 and dt == 0:
                        if tb == 0:
                            w("vector", "V", vz)
                        w("vector", "V", vdu)
                    vv(lambda e, dt=dt, idx=idx: e.tensor_tensor(
                        out=abf.v(o_u + (idx % 2) * TBB, TBB),
                        in0=abf.v(o_dz + dt * TBB, TBB),
                        in1=abf.v(o_Bb, TBB), op=ALU.mult))
                    w("vector", "S", sdA)
                    if ccpy.get(idx - 2) is not None:
                        w("vector", "S", ccpy[idx - 2])
                    vs = vv(lambda e, idx=idx, n=n, dt=dt: e.tensor_tensor_scan(
                        out=abf.v(o_h + (idx % 2) * TBB, TBB),
                        data0=a32.v(o_dA + (idx % 2) * TBB, TBB),
                        data1=abf.v(o_u + (idx % 2) * TBB, TBB),
                        initial=carry[:, n * 8 + dt:n * 8 + dt + 1],
                        op0=ALU.mult, op1=ALU.add), True)
                    if n == NST - 1:
                        w("sync", "V", vs)
                        cz.append(dma("sync", "dZL",
                                      abf.v(o_dz + dt * TBB, TBB),
                                      z_d[128 * dt:128 * (dt + 1),
                                          t0:t0 + TBB]))
                    w("scalar", "V", vs)
                    ccpy[idx] = ss(lambda e, n=n, dt=dt, idx=idx: e.copy(
                        out=carry[:, n * 8 + dt:n * 8 + dt + 1],
                        in_=abf_t[:, o_h + (idx % 2) * TBB + TBB - 1:
                                  o_h + (idx % 2) * TBB + TBB]), True)
                    vv(lambda e, idx=idx: e.tensor_tensor(
                        out=abf.v(o_u + (idx % 2) * TBB, TBB),
                        in0=abf.v(o_h + (idx % 2) * TBB, TBB),
                        in1=abf.v(o_Cb, TBB), op=ALU.mult))
                    if dev >= 3 and tb == 1 and n == 0 and dt == 0:
                        vdbg = vv(lambda e: e.engine_nop(), True)
                        w("sync", "V", vdbg)
                        dma("sync", "dDV", dv_bg[:, 0:TBB], abf.v(o_dl, TBB))
                        dma("sync", "dDV", dv_bg[:, TBB:2 * TBB], abf.v(o_dz, TBB))
                        dma("sync", "dDV", dv_bg[:, 2 * TBB:3 * TBB], abf.v(o_Bb, TBB))
                        cdbh = dma("sync", "dDV", dv_bg[:, 3 * TBB:4 * TBB],
                                   abf.v(o_h + (idx % 2) * TBB, TBB))
                        dma("sync", "dDV", dv_bg2[0:96, :], xdblb[:, 0:TBB])
                        w("vector", "dDV", cdbh)
                    if n == 0:
                        vy = vv(lambda e, dt=dt, idx=idx: e.tensor_copy(
                            a32.v(o_yb + dt * TBB, TBB),
                            abf.v(o_u + (idx % 2) * TBB, TBB)), True)
                    else:
                        vy = vv(lambda e, dt=dt, idx=idx: e.tensor_tensor(
                            out=a32.v(o_yb + dt * TBB, TBB),
                            in0=a32.v(o_yb + dt * TBB, TBB),
                            in1=abf.v(o_u + (idx % 2) * TBB, TBB), op=ALU.add),
                            True)
                    yms[(n, dt)] = vy
                    ccpy_last = ccpy[idx]
            vg = None
            for dt in range(8):
                vv(lambda e, dt=dt: e.scalar_tensor_tensor(
                    out=a32.v(o_yb + dt * TBB, TBB),
                    in0=abf.v(o_xpB + dt * TBB, TBB),
                    scalar=small[:, 16 + dt:16 + dt + 1],
                    in1=a32.v(o_yb + dt * TBB, TBB), op0=ALU.mult, op1=ALU.add))
                w("vector", "dZL", cz[7])
                vg = vv(lambda e, dt=dt: e.tensor_tensor(
                    out=abf.v(o_dl + dt * TBB, TBB),
                    in0=a32.v(o_yb + dt * TBB, TBB),
                    in1=abf.v(o_dz + dt * TBB, TBB), op=ALU.mult), True)
            if dev >= 2:
                w("sync", "V", vg)
                for dt in range(8):
                    dma("sync", "dDV", dv_y[128 * dt:128 * (dt + 1), t0:t0 + TBB],
                        a32.v(o_yb + dt * TBB, TBB))
            rs_t, rs_r0 = rs_map[tb]
            opev = {}
            ogdma = None
            for ot in range(8):
                for ch in range(2):
                    key = ot * 2 + ch
                    bank = pbank[key % 2]
                    if key >= 2:
                        w("tensor", "S", opev[key - 2])
                    tmo = None
                    for kt in range(8):
                        if key == 0 and kt == 0:
                            w("tensor", "V", vg)
                        tmo = tt(lambda e, b=bank, ot=ot, ch=ch, kt=kt: e.matmul(
                            b[:, :],
                            wo_bf[:, DM * kt + 128 * ot:DM * kt + 128 * (ot + 1)],
                            abf.v(o_dl + kt * TBB + 512 * ch, 512),
                            start=(kt == 0), stop=(kt == 7)), True)
                    w("scalar", "T", tmo)
                    if ogdma is not None and ch == 0:
                        w("scalar", "dOG", ogdma)
                    smo = ss(lambda e, b=bank, ch=ch: e.copy(
                        out=abf.v(o_og + 512 * ch, 512), in_=b[:, :]), True)
                    opev[key] = smo
                w("sync", "S", smo)
                ogdma = dma("sync", "dOG",
                            rs_t[rs_r0 + 128 * ot:rs_r0 + 128 * (ot + 1), :],
                            abf.v(o_og, TBB))
            if tb == 2:
                w("gpsimd", "dOG", p.cnt["dOG"])
                ccr[0] = p.em("gpsimd", lambda e: e.collective_compute(
                    "ReduceScatter", ALU.add, replica_groups=PAIRS,
                    ins=[rs1in], outs=[rs1out]), "CC", 1)
            if tb == 3:
                w("gpsimd", "dOG", p.cnt["dOG"])
                ccr[1] = p.em("gpsimd", lambda e: e.collective_compute(
                    "ReduceScatter", ALU.add, replica_groups=PAIRS,
                    ins=[rs2in], outs=[rs2out]), "CC", 1)

        if dev >= 2:
            w("sync", "CC", ccr[1])
            dma("sync", "dDV", dv_rs1[:, :], rs1out[:, :])
            dma("sync", "dDV", dv_rs2[:, :], rs2out[:, :])
        if upto == "B":
            for s in DMA_SEMS:
                w("sync", s, p.cnt[s])
            w("sync", "CC", p.cnt["CC"])
            return nc, p

        phase_barrier()
        a32.reset(base32)
        abf.reset(basebf)
        o_x1 = a32.alloc(8 * TBC)
        o_ts = a32.alloc(TBC)
        o_ts2 = a32.alloc(TBC)
        o_zs = a32.alloc(TBC)
        o_x2s = a32.alloc(2 * TBC)
        o_sqC = abf.alloc(1024)
        o_h0 = abf.alloc(8 * TBC)
        o_hf = abf.alloc(8 * TBC)
        o_a = abf.alloc(32 * TBC)
        o_ms = o_a
        o_res = o_a
        o_i1 = abf.alloc(TBC)
        o_m1 = abf.alloc(TBC)
        o_w1b = abf.alloc(4 * 1024)
        o_w2b = abf.alloc(2 * 1024)

        def cdump(src_off, bf=True):
            for kt in range(2):
                vx2 = vv(lambda e, kt=kt, s=src_off, bf=bf: e.tensor_copy(
                    a32.v(o_x2s + (kt % 2) * TBC, TBC),
                    abf.v(s + kt * TBC, TBC) if bf else a32.v(s + kt * TBC, TBC)),
                    True)
                w("sync", "V", vx2)
                dma("sync", "dX2", _x2o_cur[128 * kt:128 * (kt + 1),
                                            _f0_cur:_f0_cur + TBC],
                    a32.v(o_x2s + (kt % 2) * TBC, TBC))

        for rs in range(2):
            rs_out = rs1out if rs == 0 else rs2out
            x_rs = x_rs1 if rs == 0 else x_rs2
            x2o = x2a if rs == 0 else x2b
            w("sync", "CC", ccr[rs])
            for hb in range(2):
                f0 = hb * TBC
                _f0_cur = f0
                _x2o_cur = x2o
                w("sync", "V", p.cnt["V"])
                w("sync", "S", p.cnt["S"])
                cms, cx1 = [], []
                for kt in range(8):
                    cms.append(dma("sync", "dMS", abf.v(o_ms + kt * TBC, TBC),
                                   rs_out[128 * kt:128 * (kt + 1), f0:f0 + TBC]))
                for kt in range(8):
                    cx1.append(dma("sync", "dMS", a32.v(o_x1 + kt * TBC, TBC),
                                   x_rs[128 * kt:128 * (kt + 1), f0:f0 + TBC]))
                w("scalar", "dMS", cx1[-1])
                w("vector", "dMS", cx1[-1])
                w("tensor", "dMS", cx1[-1])
                if cstop == 0:
                    cdump(o_ms)
                    continue
                tbc = ln_stats(lambda kt: abf.v(o_ms + kt * TBC, TBC), TBC, o_sqC)
                vap = ln_apply(lambda kt: abf.v(o_ms + kt * TBC, TBC),
                               lambda kt: abf.v(o_h0 + kt * TBC, TBC), TBC, tbc)
                if cstop == 1:
                    cdump(o_h0)
                    continue
                vx1 = None
                for kt in range(8):
                    if cstop == 22:
                        vx1 = vv(lambda e, kt=kt: e.tensor_tensor(
                            out=a32.v(o_ts, TBC),
                            in0=a32.v(o_x1 + kt * TBC, TBC),
                            in1=abf.v(o_h0 + kt * TBC, TBC), op=ALU.add), True)
                    else:
                        vx1 = vv(lambda e, kt=kt: e.tensor_tensor(
                            out=a32.v(o_x1 + kt * TBC, TBC),
                            in0=a32.v(o_x1 + kt * TBC, TBC),
                            in1=abf.v(o_h0 + kt * TBC, TBC), op=ALU.add), True)
                if cstop in (21, 22):
                    w("sync", "V", vx1)
                    for kt in range(2):
                        vx2 = vv(lambda e, kt=kt: e.tensor_copy(
                            a32.v(o_x2s + (kt % 2) * TBC, TBC), a32.v(o_ts, TBC)),
                            True)
                        w("sync", "V", vx2)
                        dma("sync", "dX2", x2o[128 * kt:128 * (kt + 1), f0:f0 + TBC],
                            a32.v(o_x2s + (kt % 2) * TBC, TBC))
                    continue
                w("scalar", "V", vx1)
                sx1 = None
                for kt in range(8):
                    sx1 = ss(lambda e, kt=kt: e.copy(
                        out=abf.v(o_ms + kt * TBC, TBC),
                        in_=a32.v(o_x1 + kt * TBC, TBC)), True)
                w("vector", "S", sx1)
                w("tensor", "S", sx1)
                if cstop == 23:
                    for kt in range(2):
                        sx2 = ss(lambda e, kt=kt: e.copy(
                            out=a32.v(o_x2s + (kt % 2) * TBC, TBC),
                            in_=abf.v(o_ms + kt * TBC, TBC)), True)
                        w("sync", "S", sx2)
                        dma("sync", "dX2", x2o[128 * kt:128 * (kt + 1), f0:f0 + TBC],
                            a32.v(o_x2s + (kt % 2) * TBC, TBC))
                    continue
                if cstop == 2:
                    cdump(o_ms)
                    continue
                tbc = ln_stats(lambda kt: abf.v(o_ms + kt * TBC, TBC), TBC, o_sqC)
                vap = ln_apply(lambda kt: abf.v(o_ms + kt * TBC, TBC),
                               lambda kt: abf.v(o_h0 + kt * TBC, TBC), TBC, tbc)
                w("scalar", "T", tbc)
                w("scalar", "V", vap)
                ss(lambda e: e.copy(out=abf.v(o_i1, TBC), in_=pE[:, :]))
                si1 = ss(lambda e: e.copy(out=abf.v(o_m1, TBC), in_=pF[:, :]), True)
                w("vector", "S", si1)
                if cstop == 3:
                    cdump(o_h0)
                    continue
                tbc = ln_stats(lambda kt: abf.v(o_h0 + kt * TBC, TBC), TBC, o_sqC)
                vap = ln_apply(lambda kt: abf.v(o_h0 + kt * TBC, TBC),
                               lambda kt: abf.v(o_hf + kt * TBC, TBC), TBC, tbc)
                vhf = None
                for kt in range(8):
                    vhf = vv(lambda e, kt=kt: e.tensor_scalar(
                        out=abf.v(o_hf + kt * TBC, TBC),
                        in0=abf.v(o_hf + kt * TBC, TBC),
                        scalar1=small[:, 24 + kt:24 + kt + 1],
                        scalar2=small[:, 32 + kt:32 + kt + 1],
                        op0=ALU.mult, op1=ALU.add), True)
                if cstop == 4:
                    cdump(o_hf)
                    continue
                tq = {}
                sfev = None
                cw1 = {}
                for i in range(4):
                    q0, k0 = divmod(i, 8)
                    cw1[i] = dma("gpsimd", f"dF{i % 4}",
                                 abf.v(o_w1b + (i % 4) * 1024, 1024),
                                 w1_d[128 * k0:128 * (k0 + 1),
                                      1024 * q0:1024 * (q0 + 1)])
                for q in range(4):
                    if sfev is not None:
                        w("tensor", "S", sfev)
                    tlast = None
                    for kt in range(8):
                        widx = q * 8 + kt
                        w("tensor", f"dF{widx % 4}", cw1[widx])
                        if q == 0 and kt == 0:
                            w("tensor", "V", vhf)
                        for fj in range(8):
                            tlast = tt(lambda e, kt=kt, fj=fj, widx=widx: e.matmul(
                                pbank[fj][:, :],
                                abf_t[:, o_w1b + (widx % 4) * 1024 + 128 * fj:
                                      o_w1b + (widx % 4) * 1024 + 128 * (fj + 1)],
                                abf.v(o_hf + kt * TBC, TBC),
                                start=(kt == 0), stop=(kt == 7)), True)
                        tq[widx] = tlast
                        if widx + 4 < 32:
                            nq, nk = divmod(widx + 4, 8)
                            w("gpsimd", "T", tq[widx])
                            cw1[widx + 4] = dma(
                                "gpsimd", f"dF{widx % 4}",
                                abf.v(o_w1b + (widx % 4) * 1024, 1024),
                                w1_d[128 * nk:128 * (nk + 1),
                                     1024 * nq:1024 * (nq + 1)])
                    w("scalar", "T", tlast)
                    for fj in range(8):
                        ft = q * 8 + fj
                        sfev = ss(lambda e, fj=fj, ft=ft: e.activation(
                            out=abf.v(o_a + ft * TBC, TBC),
                            in_=pbank[fj][:, :], func=AF.Silu,
                            bias=small[:, 40 + ft:40 + ft + 1], scale=1.0), True)
                if cstop == 5:
                    cdump(o_a)
                    continue
                w("tensor", "S", sfev)
                t2q = {}
                t2last = None
                cw2 = {0: dma("gpsimd", "dW20", abf.v(o_w2b, 1024), w2_d[0:128, :]),
                       1: dma("gpsimd", "dW21", abf.v(o_w2b + 1024, 1024),
                              w2_d[128:256, :])}
                for ktf in range(32):
                    w("tensor", f"dW2{ktf % 2}", cw2[ktf])
                    for ot in range(8):
                        t2last = tt(lambda e, ktf=ktf, ot=ot: e.matmul(
                            pbank[ot][:, :],
                            abf_t[:, o_w2b + (ktf % 2) * 1024 + 128 * ot:
                                  o_w2b + (ktf % 2) * 1024 + 128 * (ot + 1)],
                            abf.v(o_a + ktf * TBC, TBC),
                            start=(ktf == 0), stop=(ktf == 31)), True)
                    t2q[ktf] = t2last
                    if ktf + 2 < 32:
                        w("gpsimd", "T", t2q[ktf])
                        cw2[ktf + 2] = dma(
                            "gpsimd", f"dW2{ktf % 2}",
                            abf.v(o_w2b + (ktf % 2) * 1024, 1024),
                            w2_d[128 * (ktf + 2):128 * (ktf + 3), :])
                if cstop == 6:
                    w("scalar", "T", t2last)
                    for kt in range(2):
                        sx2 = ss(lambda e, kt=kt: e.copy(
                            out=a32.v(o_x2s + (kt % 2) * TBC, TBC),
                            in_=pbank[kt][:, :]), True)
                        w("sync", "S", sx2)
                        dma("sync", "dX2", x2o[128 * kt:128 * (kt + 1), f0:f0 + TBC],
                            a32.v(o_x2s + (kt % 2) * TBC, TBC))
                    continue
                w("scalar", "T", t2last)
                vres = None
                for ot in range(8):
                    if vres is not None:
                        w("scalar", "V", vres)
                    szs = ss(lambda e, ot=ot: e.activation(
                        out=a32.v(o_zs, TBC), in_=pbank[ot][:, :], func=AF.Identity,
                        bias=small[:, 72 + ot:72 + ot + 1], scale=0.5), True)
                    vv(lambda e, ot=ot: e.tensor_tensor(
                        out=a32.v(o_ts, TBC), in0=a32.v(o_x1 + ot * TBC, TBC),
                        in1=abf.v(o_i1, TBC), op=ALU.mult))
                    vv(lambda e: e.tensor_tensor(
                        out=a32.v(o_ts2, TBC), in0=a32.v(o_ts, TBC),
                        in1=abf.v(o_m1, TBC), op=ALU.subtract))
                    w("vector", "S", szs)
                    vres = vv(lambda e, ot=ot: e.tensor_tensor(
                        out=abf.v(o_res + ot * TBC, TBC), in0=a32.v(o_ts2, TBC),
                        in1=a32.v(o_zs, TBC), op=ALU.add), True)
                tbc = ln_stats(lambda kt: abf.v(o_res + kt * TBC, TBC), TBC, o_sqC)
                vap = ln_apply(lambda kt: abf.v(o_res + kt * TBC, TBC),
                               lambda kt: abf.v(o_hf + kt * TBC, TBC), TBC, tbc)
                cst = None
                for kt in range(8):
                    if kt >= 2:
                        w("vector", "dX2", cst)
                    vx2 = vv(lambda e, kt=kt: e.tensor_tensor(
                        out=a32.v(o_x2s + (kt % 2) * TBC, TBC),
                        in0=a32.v(o_x1 + kt * TBC, TBC),
                        in1=abf.v(o_hf + kt * TBC, TBC), op=ALU.add), True)
                    w("sync", "V", vx2)
                    cst = dma("sync", "dX2",
                              x2o[128 * kt:128 * (kt + 1), f0:f0 + TBC],
                              a32.v(o_x2s + (kt % 2) * TBC, TBC))

        for s in DMA_SEMS:
            w("sync", s, p.cnt[s])
        w("sync", "CC", p.cnt["CC"])
        return nc, p


def _finish(nc, p):
    with nc.Block() as block:
        @block.sync
        def _(e):
            p.replay("sync", e)

        @block.vector
        def _(e):
            p.replay("vector", e)

        @block.scalar
        def _(e):
            p.replay("scalar", e)

        @block.tensor
        def _(e):
            p.replay("tensor", e)

        @block.gpsimd
        def _(e):
            p.replay("gpsimd", e)
    return nc


_TOK = {
    (0, 0): 1024, (0, 1): 0, (1, 0): 2048, (1, 1): 3072,
}


def make_in_maps(inputs):
    x = np.asarray(inputs["L0_attn_x"])
    Wi = np.asarray(inputs["L0_attn_Wi"])
    cw = np.asarray(inputs["L0_attn_conv_w"])
    cb = np.asarray(inputs["L0_attn_conv_b"])
    Wx = np.asarray(inputs["L0_attn_Wx"])
    Wdt = np.asarray(inputs["L0_attn_Wdt"])
    bdt = np.asarray(inputs["L0_attn_bdt"])
    Alog = np.asarray(inputs["L0_attn_A_log"])
    Dp = np.asarray(inputs["L0_attn_Dp"])
    Wo = np.asarray(inputs["L0_attn_Wo"])
    g = np.asarray(inputs["L0_ffn_ln_gamma"])
    be = np.asarray(inputs["L0_ffn_ln_beta"])
    q1 = np.asarray(inputs["L0_ffn_fc1_q"])
    s1_ = np.asarray(inputs["L0_ffn_fc1_scales"])
    b1 = np.asarray(inputs["L0_ffn_fc1_bias"])
    q2 = np.asarray(inputs["L0_ffn_fc2_q"])
    s2_ = np.asarray(inputs["L0_ffn_fc2_scales"])
    b2 = np.asarray(inputs["L0_ffn_fc2_bias"])

    def u8(q):
        v = q.view(np.uint8).reshape(q.shape[0], q.shape[1], 4)[:, :, 0]
        return np.ascontiguousarray(v)

    def tile8(vec):
        return np.ascontiguousarray(vec.reshape(8, 128).T)

    q1u, q2u = u8(q1), u8(q2)
    in_maps = []
    for c in range(8):
        b, dh = c // 2, c % 2
        sl = slice(dh * DH, (dh + 1) * DH)
        xf = np.ascontiguousarray(x[b].T.astype(np.float32))
        al = Alog[sl].reshape(8, 128, NST).transpose(1, 0, 2).reshape(128, 128)
        d = dict(
            x_f=xf,
            x_rs1=np.ascontiguousarray(xf[:, _TOK[(dh, 0)]:_TOK[(dh, 0)] + TBB]),
            x_rs2=np.ascontiguousarray(xf[:, _TOK[(dh, 1)]:_TOK[(dh, 1)] + TBB]),
            wi_t=np.ascontiguousarray(
                np.concatenate([Wi[sl], Wi[2 * DH + dh * DH:2 * DH + (dh + 1) * DH]],
                               axis=0).T),
            wx_t=np.ascontiguousarray(Wx[:, sl].T),
            wdt_t=np.ascontiguousarray(Wdt[sl].T),
            wo_t=np.ascontiguousarray(Wo[:, sl].T),
            convw=np.ascontiguousarray(cw[sl, 0, :]),
            convb=tile8(cb[sl]),
            bdt_t=tile8(bdt[sl]),
            alog=np.ascontiguousarray(al),
            dp_t=tile8(Dp[sl]),
            gam=tile8(g),
            bet=tile8(be),
            fc1b=np.ascontiguousarray(b1.reshape(32, 128).T),
            fc2b=tile8(b2),
            fc1q=q1u,
            fc2q=q2u,
            fc1s=np.ascontiguousarray(s1_),
            fc2s=np.ascontiguousarray(s2_),
            ones1=np.ones((1, 128), np.float32),
            onesc=np.ones((128, 1), np.float32),
        )
        in_maps.append(d)
    return in_maps


def run_raw(inputs, dev=None, upto=None, cstop=99):
    dev = DEV if dev is None else dev
    upto = BUILD_UPTO if upto is None else upto
    nc, p = build_nc(dev=dev, upto=upto, cstop=cstop)
    _finish(nc, p)
    in_maps = make_in_maps(inputs)
    return run_bass_kernel_spmd(nc, in_maps, core_ids=list(range(8)))


def kernel(**inputs):
    res = run_raw(inputs, dev=0, upto="C")
    out = np.empty((4, L, DM), np.float32)
    for c in range(8):
        b, dh = c // 2, c % 2
        r = res.results[c]
        t1 = _TOK[(dh, 0)]
        t2 = _TOK[(dh, 1)]
        out[b, t1:t1 + TBB, :] = np.asarray(r["x2a"]).T
        out[b, t2:t2 + TBB, :] = np.asarray(r["x2b"]).T
    return out
